# version 01
# baseline (speedup 1.0000x reference)
import numpy as np
import ml_dtypes
from contextlib import ExitStack

import concourse.bass as bass
import concourse.mybir as mybir
from concourse.bass_utils import run_bass_kernel_spmd
from concourse.bass import broadcast_tensor_aps

F32 = mybir.dt.float32
BF16 = mybir.dt.bfloat16
I32 = mybir.dt.int32
AF = mybir.ActivationFunctionType
ALU = mybir.AluOpType
AX = mybir.AxisListType
BF = ml_dtypes.bfloat16

H = 4096
D = 8192
NST = 16
RDT = 256
BATCH = 2
L = 8192
DC = 2048
NDB = DC // 128
TOWN = 2048
TC = 512
SEG = TC + 2
NQ = 2
NGRP = NST // NQ
NCORES = 8
GROUPS = [[0, 1, 2, 3], [4, 5, 6, 7]]
ENGINES = ("sync", "vector", "scalar", "tensor", "gpsimd")


def build_program(n_chunks=L // TC, taps=(), only_stage1=False, skip_ag=False, lean_init=0):
    nc = bass.Bass("TRN2", target_bir_lowering=False)
    es = ExitStack()
    E = es.enter_context

    def inp(name, shape):
        return nc.dram_tensor(name, list(shape), F32, kind="ExternalInput").ap()

    TOWNE = 128 * n_chunks
    x_own = inp("x_own", (TOWNE, H))
    w_inT = inp("w_inT", (H, 2 * DC))
    w_outT = inp("w_outT", (DC, H))
    wxT = inp("wxT", (DC, 288))
    wdtT = inp("wdtT", (RDT, DC))
    conv_w = inp("conv_w", (DC, 4))
    conv_b = inp("conv_b", (DC,))
    a_par = inp("a_par", (DC, NST))
    d_par = inp("d_par", (DC,))
    dt_bias = inp("dt_bias", (DC,))
    dt_ln_w = inp("dt_ln_w", (RDT,))
    b_ln_w = inp("b_ln_w", (NST,))
    c_ln_w = inp("c_ln_w", (NST,))
    ffn_aT = inp("ffn_aT", (H, 1024))
    ffn_bT = inp("ffn_bT", (1024, H))
    ffn_wrT = inp("ffn_wrT", (H, 64))

    y_out = nc.dram_tensor("y_out", [TOWNE, H], F32, kind="ExternalOutput").ap()

    w_inT_bf = nc.dram_tensor("w_inT_bf", [H, 2 * DC], BF16).ap()
    w_outT_bf = nc.dram_tensor("w_outT_bf", [DC, H], BF16).ap()
    ffn_aT_bf = nc.dram_tensor("ffn_aT_bf", [H, 1024], BF16).ap()
    ffn_bT_bf = nc.dram_tensor("ffn_bT_bf", [1024, H], BF16).ap()
    ffn_wrT_bf = nc.dram_tensor("ffn_wrT_bf", [H, 64], BF16).ap()
    ag_in = nc.dram_tensor("ag_in", [H, TOWNE], BF16).ap()
    ag_out = nc.dram_tensor("ag_out", [4 * H, TOWNE], BF16).ap()
    ssm_in = [nc.dram_tensor(f"ssm_in{c}", [288, TC], F32).ap() for c in range(n_chunks)]
    ssm_out = [nc.dram_tensor(f"ssm_out{c}", [288, TC], F32).ap() for c in range(n_chunks)]
    attn_in = [nc.dram_tensor(f"attn_in{c}", [4 * H, 128], F32).ap() for c in range(n_chunks)]
    attn_rs = [nc.dram_tensor(f"attn_rs{c}", [H, 128], F32).ap() for c in range(n_chunks)]

    tap_aps = {}
    tap_shapes = {
        "ub": ((128, NDB, TC), BF16), "delta": ((128, NDB, TC), F32),
        "yg": ((128, NDB, TC), BF16), "sg": ((128, NDB, TC), BF16),
        "xt": ((128, 32, TC), BF16), "carry": ((128, NDB, NST), F32),
        "dtn": ((128, 2, TC), F32), "bu": ((16, TC), F32), "cu": ((16, TC), F32),
        "ulin": ((128, NDB, TC + 3), F32),
        "attnq": ((H, 128), F32),
        "attnp": ((4 * H, 128), F32),
        "woutbf": ((DC, 128), BF16),
        "psout": ((128, TC), F32),
        "slab0": ((128, 8, 128), BF16),
        "sct": ((128, 2, 64), F32),
        "gts": ((128, 2, 64), BF16),
        "grep": ((128, 8, 256), BF16),
        "hg": ((128, 8, 256), BF16),
        "yg2": ((128, NDB, TC), BF16),
        "agin": ((H, min(TC, 128 * n_chunks)), BF16),
        "agout": ((H, min(TC, 128 * n_chunks)), BF16),
    }
    for t in taps:
        shp, dt = tap_shapes[t]
        tap_aps[t] = nc.dram_tensor("tap_" + t, list(shp), dt,
                                    kind="ExternalOutput").ap()

    plan = {e: [] for e in ENGINES}

    def em(engine, fn):
        plan[engine].append(fn)

    class Sem:
        __slots__ = ("h", "n", "name")

        def __init__(self, name):
            self.h = E(nc.semaphore(name))
            self.n = 0
            self.name = name

    last_wait = {e: {} for e in ENGINES}

    def wait(engine, sem, val=None):
        v = sem.n if val is None else val
        if v <= 0:
            return
        lw = last_wait[engine]
        if lw.get(sem.name, 0) >= v:
            return
        lw[sem.name] = v
        em(engine, lambda eng, s=sem.h, vv=v: eng.wait_ge(s, vv))

    def run(engine, fn, inc=None, amt=1):
        if inc is None:
            em(engine, fn)
            return None
        inc.n += amt
        em(engine, lambda eng, f=fn, s=inc.h, a=amt: f(eng).then_inc(s, a))
        return inc.n

    s_dma = Sem("s_dma")
    s_wprep = Sem("s_wprep")
    s_cc = Sem("s_cc")
    s_pe = Sem("s_pe")
    s_dv = Sem("s_dv")
    s_ac = Sem("s_ac")
    s_gp = Sem("s_gp")
    s_st = Sem("s_st")
    s_xt = Sem("s_xt")
    s_win = Sem("s_win")
    s_wout = Sem("s_wout")
    s_shtl = Sem("s_shtl")

    def sbt(name, shape, dt):
        return E(nc.sbuf_tensor(name, list(shape), dt))

    esA = ExitStack()

    def sbtA(name, shape, dt):
        return esA.enter_context(nc.sbuf_tensor(name, list(shape), dt))

    A_sb = sbt("A_sb", [128, NDB, NST], F32)
    cw_sb = sbt("cw_sb", [128, NDB, 4], F32)
    cb_sb = sbt("cb_sb", [128, NDB], F32)
    dp_sb = sbt("dp_sb", [128, NDB], F32)
    dtbias_sb = sbt("dtbias_sb", [128, NDB], F32)
    wdt_sb = sbt("wdt_sb", [128, 2, DC], BF16)
    wx_sb = sbt("wx_sb", [128, NDB, 288], BF16)
    selmat = sbt("selmat", [16, NST, 128], BF16)
    ident = sbt("ident", [128, 128], BF16)
    identf = sbt("identf", [128, 128], F32)
    ones1f = sbt("ones1f", [1, 128], F32)
    ones16x = sbt("ones16x", [16, 128], F32)
    onescol = sbt("onescol", [128, 1], BF16)
    onescol16 = sbt("onescol16", [16, 1], BF16)
    ones16f = sbt("ones16f", [1, 16], F32)
    p8mats = sbt("p8mats", [64, 8, 128], BF16)
    iotaf = sbt("iotaf", [128, 128], F32)
    pidxf = sbt("pidxf", [128, 1], F32)
    smalli = sbt("smalli", [128, 128], I32)
    blw_sb = sbt("blw_sb", [16, 2], F32)
    mask16 = sbt("mask16", [16, 2], F32)
    dpdiag = sbt("dpdiag", [128, NDB, 128], BF16)
    cst_sb = sbt("cst_sb", [128, 8], F32)

    arena = sbtA("arena", [128, 13440], F32)
    _arb = arena[:].bitcast(BF16)
    xt_sb = _arb[:, 0:16384].rearrange("p (a t) -> p a t", a=32)
    brep_sb = _arb[:, 0:8192].rearrange("p (n t) -> p n t", n=NST)
    crep_sb = _arb[:, 8192:16384].rearrange("p (n t) -> p n t", n=NST)
    dax_sb = arena[:, 8192:8192 + 2 * NQ * SEG].rearrange(
        "p (b q s) -> p b q s", b=2, q=NQ)
    vbs_sb = _arb[:, 20608:20608 + 2 * NQ * SEG].rearrange(
        "p (b q s) -> p b q s", b=2, q=NQ)
    ss_sb = _arb[:, 22720:22720 + 2 * NQ * SEG].rearrange(
        "p (b q s) -> p b q s", b=2, q=NQ)
    ud_sb = sbtA("ud_sb", [128, NDB, TC + 3], F32)
    halo_sb = sbtA("halo_sb", [128, NDB, 3], F32)
    acc_sb = sbtA("acc_sb", [128, TC], F32)
    sg_sb = sbtA("sg_sb", [128, NDB, TC], BF16)
    ub_sb = sbtA("ub_sb", [128, NDB, TC], BF16)
    v_db = sbtA("v_db", [128, 2, TC], BF16)
    yg_sb = sbtA("yg_sb", [128, NDB, TC], BF16)
    dt_sum = sbtA("dt_sum", [128, 2, TC], F32)
    dtq_sb = sbtA("dtq_sb", [128, 2, TC], BF16)
    dtn_sb = sbtA("dtn_sb", [128, 2, TC], BF16)
    b_sum = sbtA("b_sum", [16, TC], F32)
    c_sum = sbtA("c_sum", [16, TC], F32)
    bq_sb = sbtA("bq_sb", [16, 2, TC], BF16)
    bu_sb = sbtA("bu_sb", [16, TC], BF16)
    cu_sb = sbtA("cu_sb", [16, TC], BF16)
    relu_sb = sbtA("relu_sb", [128, TC], F32)
    abs_sb = sbtA("abs_sb", [128, 2, TC], F32)
    rsdt_sb = abs_sb[0:1, 0, :]
    rsb_sb = abs_sb[0:1, 1, :]
    rsc_sb = relu_sb[0:1, :]
    carry = sbtA("carry", [128, NDB, NST], F32)
    sc_sb = sbtA("sc_sb", [128, 2, NQ, TC], BF16)
    win_slab = sbtA("win_slab", [128, 2, 16, 128], BF16)
    wout_slab = sbtA("wout_slab", [128, 2, 8, 128], BF16)

    attn_st = ud_sb[:, 0:2, 0:TC]
    ps = [E(nc.psum_tensor(f"ps{i}", [128, TC], F32)) for i in range(8)]
    bank_guard = {}

    def pe_use_bank(bank):
        g = bank_guard.get(bank)
        if g is not None:
            wait("tensor", g[0], g[1])

    if lean_init < 3:
     run("sync", lambda e: e.dma_start(out=A_sb[:], in_=a_par.rearrange("(p db) n -> p db n", p=128)), inc=s_dma, amt=16)
    run("sync", lambda e: e.dma_start(out=cw_sb[:], in_=conv_w.rearrange("(p db) k -> p db k", p=128)), inc=s_dma, amt=16)
    run("sync", lambda e: e.dma_start(out=cb_sb[:], in_=conv_b.rearrange("(p db) -> p db", p=128)), inc=s_dma, amt=16)
    run("sync", lambda e: e.dma_start(out=dp_sb[:], in_=d_par.rearrange("(p db) -> p db", p=128)), inc=s_dma, amt=16)
    run("sync", lambda e: e.dma_start(out=dtbias_sb[:], in_=dt_bias.rearrange("(p db) -> p db", p=128)), inc=s_dma, amt=16)
    run("sync", lambda e: e.dma_start(out=blw_sb[:, 0:1], in_=b_ln_w[:, None]), inc=s_dma, amt=16)
    run("sync", lambda e: e.dma_start(out=blw_sb[:, 1:2], in_=c_ln_w[:, None]), inc=s_dma, amt=16)
    wdt_stage = ud_sb[:].rearrange("p a t -> p (a t)")[:, 0:2 * DC].rearrange(
        "p (rb d) -> p rb d", rb=2)
    run("sync", lambda e: e.dma_start(out=wdt_stage, in_=wdtT.rearrange("(p rb) d -> p rb d", p=128)), inc=s_dma, amt=16)
    dtw_stage = acc_sb[:, 0:2]
    run("sync", lambda e: e.dma_start(out=dtw_stage, in_=dt_ln_w.rearrange("(p rb) -> p rb", p=128)), inc=s_dma, amt=16)
    wx_stage1 = yg_sb[:].rearrange("p a t -> p (a t)").bitcast(F32)[:, 0:12 * 288].rearrange(
        "p (db o) -> p db o", db=12)
    wx_stage2 = ud_sb[:].rearrange("p a t -> p (a t)")[:, 4096:4096 + 4 * 288].rearrange(
        "p (db o) -> p db o", db=4)
    run("sync", lambda e: e.dma_start(
        out=wx_stage1, in_=wxT[0:12 * 128 if False else 0:1536, :].rearrange(
            "(p db) o -> p db o", p=128) if False else wxT.rearrange(
            "(p db) o -> p db o", p=128)[:, 0:12, :]), inc=s_dma, amt=16)
    run("sync", lambda e: e.dma_start(
        out=wx_stage2, in_=wxT.rearrange("(p db) o -> p db o", p=128)[:, 12:16, :]), inc=s_dma, amt=16)
    n_dma0 = s_dma.n

    n_wprep = 0

    run("gpsimd", lambda e: e.iota(smalli[:], pattern=[[1, 128]], base=0,
                                   channel_multiplier=0), inc=s_gp)
    g1 = s_gp.n
    wait("vector", s_gp, g1)
    run("vector", lambda e: e.tensor_copy(out=iotaf[:], in_=smalli[:]), inc=s_dv)
    v1 = s_dv.n
    wait("gpsimd", s_dv, v1)
    run("gpsimd", lambda e: e.iota(smalli[:, 0:1], pattern=[[0, 1]], base=0,
                                   channel_multiplier=1), inc=s_gp)
    g2 = s_gp.n
    wait("vector", s_gp, g2)
    wait("vector", s_dv, v1)
    run("vector", lambda e: e.tensor_copy(out=pidxf[:], in_=smalli[:, 0:1]), inc=s_dv)
    v2 = s_dv.n
    wait("vector", s_dv, v2)
    run("vector", lambda e: e.tensor_single_scalar(out=ident[:], in_=iotaf[:], scalar=pidxf[:, 0:1], op=ALU.is_equal), inc=s_dv)
    run("vector", lambda e: e.tensor_single_scalar(out=identf[:], in_=iotaf[:], scalar=pidxf[:, 0:1], op=ALU.is_equal), inc=s_dv)
    run("vector", lambda e: e.memset(ones1f[:], 1.0), inc=s_dv)
    run("vector", lambda e: e.memset(ones16x[:], 1.0), inc=s_dv)
    run("vector", lambda e: e.memset(onescol[:], 1.0), inc=s_dv)
    run("vector", lambda e: e.memset(onescol16[:], 1.0), inc=s_dv)
    run("vector", lambda e: e.memset(ones16f[:], 1.0), inc=s_dv)
    for col, val in ((0, 1e-5), (1, 1e-6), (2, 1.0 / RDT), (3, 1.0 / NST),
                     (4, 1.0), (5, -1.0)):
        run("vector", lambda e, col=col, val=val: e.memset(cst_sb[:, col:col + 1], val), inc=s_dv)
    run("vector", lambda e: e.tensor_scalar(out=abs_sb[0:64, 0, 0:128], in0=iotaf[0:64, :], scalar1=7.5, scalar2=1.0 / 16.0, op0=ALU.subtract, op1=ALU.mult), inc=s_dv)
    v3 = s_dv.n
    wait("vector", s_dv, v3)
    run("vector", lambda e: e.tensor_copy(out=smalli[0:64, 0:128], in_=abs_sb[0:64, 0, 0:128]), inc=s_dv)
    v4 = s_dv.n
    wait("vector", s_dv, v4)
    run("vector", lambda e: e.tensor_copy(out=abs_sb[0:64, 0, 0:128], in_=smalli[0:64, 0:128]), inc=s_dv)
    v5 = s_dv.n
    wait("vector", s_dv, v5)
    for kb in range(8):
        run("vector", lambda e, kb=kb: e.tensor_scalar(
            out=p8mats[:, kb, :], in0=abs_sb[0:64, 0, 0:128],
            scalar1=float(8 * kb), scalar2=pidxf[0:64, 0:1],
            op0=ALU.add, op1=ALU.is_equal), inc=s_dv)
    wait("vector", s_dma, n_dma0)
    for n in range(NST):
        run("vector", lambda e, n=n: e.tensor_single_scalar(
            out=mask16[:, 0:1], in_=pidxf[0:16, 0:1], scalar=float(n), op=ALU.is_equal), inc=s_dv)
        vm = s_dv.n
        wait("vector", s_dv, vm)
        run("vector", lambda e, n=n: e.tensor_scalar(
            out=selmat[:, n, :], in0=ones16x[:], scalar1=mask16[:, 0:1],
            scalar2=None, op0=ALU.mult), inc=s_dv)
    for rb in range(2):
        run("vector", lambda e, rb=rb: e.tensor_scalar(
            out=wdt_sb[:, rb, :], in0=wdt_stage[:, rb, :],
            scalar1=dtw_stage[:, rb:rb + 1], scalar2=None, op0=ALU.mult), inc=s_dv)
    run("vector", lambda e: e.tensor_copy(out=wx_sb[:, 0:12, :], in_=wx_stage1), inc=s_dv)
    run("vector", lambda e: e.tensor_copy(out=wx_sb[:, 12:16, :], in_=wx_stage2), inc=s_dv)
    for db in range(NDB):
        run("vector", lambda e, db=db: e.tensor_scalar(
            out=dpdiag[:, db, :], in0=ident[:], scalar1=dp_sb[:, db:db + 1],
            scalar2=None, op0=ALU.mult), inc=s_dv)
    run("vector", lambda e: e.memset(carry[:], 0.0), inc=s_dv)
    run("vector", lambda e: e.memset(halo_sb[:], 0.0), inc=s_dv)
    v_wfold = s_dv.n
    wait("vector", s_dv, v_wfold)
    run("vector", lambda e: e.memset(dax_sb[:, :, :, 0:1], 0.0), inc=s_dv)
    run("vector", lambda e: e.memset(dax_sb[:, :, :, 1:2], 1.0), inc=s_dv)
    run("vector", lambda e: e.memset(vbs_sb[:, :, :, 1:2], 0.0), inc=s_dv)
    v_init = s_dv.n

    lnx = arena[:, 0:8192].rearrange("p (a t) -> p a t", a=2)
    lnb = sg_sb[:].rearrange("p a t -> p (a t)").rearrange("p (a t) -> p a t", a=2)
    sqx = yg_sb[:].rearrange("p a t -> p (a t)").bitcast(F32)
    xTq = ub_sb[:].rearrange("p a t -> p (a t)").rearrange("p (a t) -> p a t", a=32)
    with ExitStack() as es1:
        stat = es1.enter_context(nc.sbuf_tensor("stat", [128, 8], F32))
        wait("sync", s_dv, v_init)
        wait("scalar", s_dv, v_init)
        n_tiles = TOWNE // 128
        n_groups = (n_tiles + 1) // 2
        for g in range(n_groups):
            gsz = min(2, n_tiles - 2 * g)
            if g > 0:
                wait("sync", s_pe, s_pe.n)
                wait("sync", s_dma, s_dma.n)
            for i in range(gsz):
                ti = 2 * g + i
                run("sync", lambda e, ti=ti, i=i: e.dma_start(
                    out=lnx[:, i, :], in_=x_own[128 * ti:128 * (ti + 1), :]),
                    inc=s_dma, amt=16)
            wait("vector", s_dma, s_dma.n)
            wait("scalar", s_dma, s_dma.n)
            wait("vector", s_dv, v_init)
            for i in range(gsz):
                run("vector", lambda e, i=i: e.tensor_reduce(
                    out=stat[:, 0:1], in_=lnx[:, i, :], axis=AX.X, op=ALU.add), inc=s_dv)
                run("scalar", lambda e, i=i: e.activation(
                    out=sqx[:], in_=lnx[:, i, :], func=AF.Square), inc=s_ac)
                va = s_ac.n
                wait("vector", s_ac, va)
                run("vector", lambda e: e.tensor_reduce(
                    out=stat[:, 1:2], in_=sqx[:], axis=AX.X, op=ALU.add), inc=s_dv)
                vr = s_dv.n
                wait("vector", s_dv, vr)
                run("vector", lambda e: e.tensor_scalar(
                    out=stat[:, 2:3], in0=stat[:, 0:1], scalar1=1.0 / H, scalar2=None, op0=ALU.mult), inc=s_dv)
                run("vector", lambda e: e.tensor_scalar(
                    out=stat[:, 4:5], in0=stat[:, 1:2], scalar1=1.0 / H, scalar2=None, op0=ALU.mult), inc=s_dv)
                vr2 = s_dv.n
                wait("vector", s_dv, vr2)
                run("vector", lambda e: e.tensor_tensor(
                    out=stat[:, 3:4], in0=stat[:, 2:3], in1=stat[:, 2:3], op=ALU.mult), inc=s_dv)
                vr3 = s_dv.n
                wait("vector", s_dv, vr3)
                run("vector", lambda e: e.tensor_tensor(
                    out=stat[:, 5:6], in0=stat[:, 4:5], in1=stat[:, 3:4], op=ALU.subtract), inc=s_dv)
                vr4 = s_dv.n
                wait("scalar", s_dv, vr4)
                run("scalar", lambda e: e.activation(
                    out=stat[:, 6:7], in_=stat[:, 5:6], func=AF.Sqrt, bias=cst_sb[:, 0:1]), inc=s_ac)
                va2 = s_ac.n
                wait("vector", s_ac, va2)
                run("vector", lambda e: e.reciprocal(out=stat[:, 7:8], in_=stat[:, 6:7]), inc=s_dv)
                vr5 = s_dv.n
                wait("vector", s_dv, vr5)
                run("vector", lambda e, i=i: e.tensor_scalar(
                    out=lnb[:, i, :], in0=lnx[:, i, :], scalar1=stat[:, 2:3],
                    scalar2=stat[:, 7:8], op0=ALU.subtract, op1=ALU.mult), inc=s_dv)
            v_ln = s_dv.n
            wait("tensor", s_dv, v_ln)
            for j in range(32):
                bank = j % 2
                pe_use_bank(bank)
                for i in range(gsz):
                    run("tensor", lambda e, i=i, j=j, bank=bank: e.transpose(
                        ps[bank][:].bitcast(BF16)[:, 128 * i:128 * (i + 1)],
                        lnb[:, i, 128 * j:128 * (j + 1)], ident[:]),
                        inc=s_pe, amt=(1 if i == gsz - 1 else 0))
                pj = s_pe.n
                wait("scalar", s_pe, pj)
                ev = run("scalar", lambda e, j=j, bank=bank, gsz=gsz: e.activation(
                    out=xTq[:, j, 0:128 * gsz],
                    in_=ps[bank][:].bitcast(BF16)[:, 0:128 * gsz],
                    func=AF.Copy), inc=s_ac)
                bank_guard[bank] = (s_ac, ev)
            wait("sync", s_ac, s_ac.n)
            run("sync", lambda e, g=g, gsz=gsz: e.dma_start(
                out=ag_in[:, 256 * g:256 * g + 128 * gsz].rearrange("(hb p) t -> p hb t", p=128),
                in_=xTq[:, :, 0:128 * gsz]), inc=s_dma, amt=16)
        wait("gpsimd", s_dma, s_dma.n)
        if not skip_ag:
            run("gpsimd", lambda e: e.collective_compute(
                "AllGather", ALU.bypass, replica_groups=GROUPS,
                ins=[ag_in.opt()], outs=[ag_out.opt()]), inc=s_cc)
        cc_ag = s_cc.n
        for dst, src in ((w_inT_bf, w_inT), (w_outT_bf, w_outT),
                         (ffn_aT_bf, ffn_aT), (ffn_bT_bf, ffn_bT),
                         (ffn_wrT_bf, ffn_wrT)):
            run("gpsimd", lambda e, d=dst, s=src: e.dma_start(out=d, in_=s),
                inc=s_wprep, amt=16)
        if "agin" in taps:
            wait("sync", s_dma, s_dma.n)
            run("sync", lambda e: e.dma_start(
                out=tap_aps["agin"], in_=ag_in[:, 0:min(TC, TOWNE)]), inc=s_st, amt=16)
        if "agout" in taps:
            wait("sync", s_cc, cc_ag)
            run("sync", lambda e: e.dma_start(
                out=tap_aps["agout"], in_=ag_out[0:H, 0:min(TC, TOWNE)]), inc=s_st, amt=16)

    if only_stage1:
        wait("sync", s_cc, cc_ag)
        wait("sync", s_st, s_st.n)
        with nc.Block() as block:
            @block.sync
            def _(eng):
                for f in plan["sync"]:
                    f(eng)

            @block.vector
            def _(eng):
                for f in plan["vector"]:
                    f(eng)

            @block.scalar
            def _(eng):
                for f in plan["scalar"]:
                    f(eng)

            @block.tensor
            def _(eng):
                for f in plan["tensor"]:
                    f(eng)

            @block.gpsimd
            def _(eng):
                for f in plan["gpsimd"]:
                    f(eng)
        esA.close()
        es.close()
        return nc

    def load_win_half(cb, half, buf):
        def f(e):
            return e.dma_start(
                out=win_slab[:, buf, :, :],
                in_=w_inT_bf[2048 * half:2048 * (half + 1),
                             128 * cb:128 * (cb + 1)].rearrange(
                    "(hb p) c -> p hb c", p=128))
        return f

    def load_wout_half(hb, half, buf):
        def f(e):
            return e.dma_start(
                out=wout_slab[:, buf, :, :],
                in_=w_outT_bf[1024 * half:1024 * (half + 1),
                              128 * hb:128 * (hb + 1)].rearrange(
                    "(db p) c -> p db c", p=128))
        return f

    wait("sync", s_wprep, 32)
    win_vals = []
    wout_vals = []
    win_vals.append(run("sync", load_win_half(0, 0, 0), inc=s_win, amt=16))
    win_vals.append(run("sync", load_win_half(0, 1, 1), inc=s_win, amt=16))
    wout_vals.append(run("sync", load_wout_half(0, 0, 0), inc=s_wout, amt=16))
    wout_vals.append(run("sync", load_wout_half(0, 1, 1), inc=s_wout, amt=16))

    tap_done = set()
    stage_free = {0: (s_gp, 0), 1: (s_gp, 0)}
    sc_free = {0: (s_pe, 0), 1: (s_pe, 0)}
    pe_win_done_prev = 0
    v_yg_prev = 0
    gp_cmul_prev = 0
    vdb_free = {0: 0, 1: 0}

    for c in range(n_chunks):
        wait("sync", s_cc, cc_ag)
        if c > 0:
            wait("sync", s_pe, pe_win_done_prev)
            wait("sync", s_dv, v_yg_prev)
            wait("sync", s_gp, gp_cmul_prev)
        for q in range(4):
            run("sync", lambda e, q=q, c=c: e.dma_start(
                out=xt_sb[:, :, 128 * q:128 * (q + 1)],
                in_=ag_out[H * q:H * (q + 1), 128 * c:128 * (c + 1)].rearrange(
                    "(hb p) t -> p hb t", p=128)), inc=s_xt, amt=16)
        dma_xt = s_xt.n

        wait("vector", s_dv, v_init)
        wait("scalar", s_dv, v_yg_prev)
        wait("scalar", s_gp, gp_cmul_prev)

        wait("tensor", s_xt, dma_xt)
        for cb in range(32):
            bank = cb % 2
            pe_use_bank(bank)
            for half in range(2):
                k = 64 * c + 2 * cb + half
                buf = k % 2
                wait("tensor", s_win, win_vals[k])
                for hb in range(16):
                    run("tensor", lambda e, hb=hb, half=half, buf=buf, bank=bank: e.matmul(
                        ps[bank][:], win_slab[:, buf, hb, :],
                        xt_sb[:, 16 * half + hb, :],
                        start=(half == 0 and hb == 0), stop=(half == 1 and hb == 15)),
                        inc=(s_pe if hb == 15 else None))
                half_done = s_pe.n
                nxt = k + 2
                if nxt < 64 * n_chunks:
                    ncb, nhalf = (nxt % 64) // 2, nxt % 2
                    wait("sync", s_pe, half_done)
                    win_vals.append(run("sync", load_win_half(ncb, nhalf, buf),
                                        inc=s_win, amt=16))
            pe_done = s_pe.n
            wait("scalar", s_pe, pe_done)
            if cb < 16:
                ev = run("scalar", lambda e, cb=cb, bank=bank: e.activation(
                    out=ud_sb[:, cb, 3:3 + TC], in_=ps[bank][:], func=AF.Copy), inc=s_ac)
            else:
                ev = run("scalar", lambda e, cb=cb, bank=bank: e.activation(
                    out=sg_sb[:, cb - 16, :], in_=ps[bank][:], func=AF.Silu), inc=s_ac)
            bank_guard[bank] = (s_ac, ev)
        pe_win_done_prev = s_pe.n
        wait("vector", s_ac, s_ac.n)
        run("vector", lambda e: e.tensor_copy(out=ud_sb[:, :, 0:3], in_=halo_sb[:]), inc=s_dv)
        v_halo = s_dv.n

        if c == 0 and "ulin" in taps and "ulin" not in tap_done:
            tap_done.add("ulin")
            wait("sync", s_dv, v_halo)
            run("sync", lambda e: e.dma_start(out=tap_aps["ulin"], in_=ud_sb[:]), inc=s_st, amt=16)
            em("vector", lambda e, s=s_st.h, vv=s_st.n: e.wait_ge(s, vv))

        prev_silu = 0
        for db in range(NDB):
            wait("vector", s_dv, v_halo)
            if prev_silu:
                wait("vector", s_ac, prev_silu)
            run("vector", lambda e, db=db: e.tensor_scalar(
                out=acc_sb[:], in0=ud_sb[:, db, 0:TC],
                scalar1=cw_sb[:, db, 0:1], scalar2=None, op0=ALU.mult), inc=s_dv)
            for kk in range(1, 4):
                vp = s_dv.n
                wait("vector", s_dv, vp)
                run("vector", lambda e, db=db, kk=kk: e.scalar_tensor_tensor(
                    out=acc_sb[:], in0=ud_sb[:, db, kk:kk + TC],
                    scalar=cw_sb[:, db, kk:kk + 1], in1=acc_sb[:],
                    op0=ALU.mult, op1=ALU.add), inc=s_dv)
            v_conv = s_dv.n
            wait("scalar", s_dv, v_conv)
            run("scalar", lambda e, db=db: e.activation(
                out=ub_sb[:, db, :], in_=acc_sb[:], func=AF.Silu,
                bias=cb_sb[:, db:db + 1]), inc=s_ac)
            ac_u = s_ac.n
            prev_silu = ac_u
        ac_ub = s_ac.n
        wait("vector", s_ac, ac_ub)
        run("vector", lambda e: e.tensor_copy(out=halo_sb[:], in_=ud_sb[:, :, TC:TC + 3]), inc=s_dv)
        v_hsave = s_dv.n

        if c == 0:
            for t, buf, gate in (("ub", ub_sb, (s_ac, ac_ub)),
                                 ("sg", sg_sb, (s_ac, ac_ub)),
                                 ("xt", xt_sb, (s_xt, dma_xt))):
                if t in taps and t not in tap_done:
                    tap_done.add(t)
                    wait("sync", gate[0], gate[1])
                    run("sync", lambda e, t=t, buf=buf: e.dma_start(
                        out=tap_aps[t], in_=buf[:]), inc=s_st, amt=16)

        ssm_banks = (4, 5, 6, 7)
        for ob in range(4):
            bank = ssm_banks[ob]
            pe_use_bank(bank)
            wait("tensor", s_ac, ac_ub)
            if ob < 2:
                sl = lambda db, ob=ob: wx_sb[:, db, 128 * ob:128 * (ob + 1)]
                op_full = True
            elif ob == 2:
                sl = lambda db: wx_sb[:, db, 256:272]
                op_full = False
            else:
                sl = lambda db: wx_sb[:, db, 272:288]
                op_full = False
            for db in range(NDB):
                run("tensor", lambda e, db=db, sl=sl, bank=bank, op_full=op_full: e.matmul(
                    ps[bank][:] if op_full else ps[bank][0:16, :],
                    sl(db), ub_sb[:, db, :],
                    start=(db == 0), stop=(db == NDB - 1)),
                    inc=(s_pe if db == NDB - 1 else None))
            pe_s = s_pe.n
            wait("scalar", s_pe, pe_s)
            if c > 0:
                wait("scalar", s_shtl, ssm_store_done)
            if ob < 2:
                ev = run("scalar", lambda e, ob=ob, bank=bank: e.activation(
                    out=dt_sum[:, ob, :], in_=ps[bank][:], func=AF.Copy), inc=s_ac)
            elif ob == 2:
                ev = run("scalar", lambda e, bank=bank: e.activation(
                    out=b_sum[:], in_=ps[bank][0:16, :], func=AF.Copy), inc=s_ac)
            else:
                ev = run("scalar", lambda e, bank=bank: e.activation(
                    out=c_sum[:], in_=ps[bank][0:16, :], func=AF.Copy), inc=s_ac)
            bank_guard[bank] = (s_ac, ev)
        ac_ssm = s_ac.n
        wait("sync", s_ac, ac_ssm)
        run("sync", lambda e, c=c: e.dma_start(
            out=ssm_in[c][0:256, :].rearrange("(rb p) t -> p rb t", p=128),
            in_=dt_sum[:]), inc=s_shtl, amt=16)
        run("sync", lambda e, c=c: e.dma_start(out=ssm_in[c][256:272, :], in_=b_sum[:]), inc=s_shtl, amt=16)
        run("sync", lambda e, c=c: e.dma_start(out=ssm_in[c][272:288, :], in_=c_sum[:]), inc=s_shtl, amt=16)
        ssm_store_done = s_shtl.n
        wait("gpsimd", s_shtl, ssm_store_done)
        run("gpsimd", lambda e, c=c: e.collective_compute(
            "AllReduce", ALU.add, replica_groups=GROUPS,
            ins=[ssm_in[c].opt()], outs=[ssm_out[c].opt()]), inc=s_cc)
        wait("sync", s_cc, s_cc.n)
        run("sync", lambda e, c=c: e.dma_start(
            out=dt_sum[:], in_=ssm_out[c][0:256, :].rearrange("(rb p) t -> p rb t", p=128)), inc=s_shtl, amt=16)
        run("sync", lambda e, c=c: e.dma_start(out=b_sum[:], in_=ssm_out[c][256:272, :]), inc=s_shtl, amt=16)
        run("sync", lambda e, c=c: e.dma_start(out=c_sum[:], in_=ssm_out[c][272:288, :]), inc=s_shtl, amt=16)
        dma_ssm = s_shtl.n

        wait("scalar", s_shtl, dma_ssm)
        for rb in range(2):
            run("scalar", lambda e, rb=rb: e.activation(
                out=dtq_sb[:, rb, :], in_=dt_sum[:, rb, :], func=AF.Square), inc=s_ac)
        run("scalar", lambda e: e.activation(out=bq_sb[:, 0, :], in_=b_sum[:], func=AF.Square), inc=s_ac)
        run("scalar", lambda e: e.activation(out=bq_sb[:, 1, :], in_=c_sum[:], func=AF.Square), inc=s_ac)
        ac_sq = s_ac.n
        for bk in (0, 1, 2):
            pe_use_bank(bk)
        wait("tensor", s_ac, ac_sq)
        for rb in range(2):
            run("tensor", lambda e, rb=rb: e.matmul(
                ps[0][0:1, :], onescol[:], dtq_sb[:, rb, :],
                start=(rb == 0), stop=(rb == 1)), inc=(s_pe if rb == 1 else None))
        run("tensor", lambda e: e.matmul(ps[1][0:1, :], onescol16[:], bq_sb[0:16, 0, :], start=True, stop=True), inc=s_pe)
        run("tensor", lambda e: e.matmul(ps[2][0:1, :], onescol16[:], bq_sb[0:16, 1, :], start=True, stop=True), inc=s_pe)
        pe_sq = s_pe.n
        wait("scalar", s_pe, pe_sq)
        run("scalar", lambda e: e.activation(out=rsdt_sb, in_=ps[0][0:1, :], func=AF.Sqrt, scale=cst_sb[0:1, 2:3], bias=cst_sb[0:1, 1:2]), inc=s_ac)
        run("scalar", lambda e: e.activation(out=rsb_sb, in_=ps[1][0:1, :], func=AF.Sqrt, scale=cst_sb[0:1, 3:4], bias=cst_sb[0:1, 1:2]), inc=s_ac)
        run("scalar", lambda e: e.activation(out=rsc_sb, in_=ps[2][0:1, :], func=AF.Sqrt, scale=cst_sb[0:1, 3:4], bias=cst_sb[0:1, 1:2]), inc=s_ac)
        ac_rs = s_ac.n
        for bk in (0, 1, 2):
            bank_guard[bk] = (s_ac, ac_rs)
        wait("vector", s_ac, ac_rs)
        run("vector", lambda e: e.reciprocal(out=rsdt_sb, in_=rsdt_sb), inc=s_dv)
        run("vector", lambda e: e.reciprocal(out=rsb_sb, in_=rsb_sb), inc=s_dv)
        run("vector", lambda e: e.reciprocal(out=rsc_sb, in_=rsc_sb), inc=s_dv)
        v_rs = s_dv.n
        for bk in (0, 1, 2):
            pe_use_bank(bk)
        wait("tensor", s_dv, v_rs)
        run("tensor", lambda e: e.matmul(ps[0][:], ones1f[:], rsdt_sb, start=True, stop=True), inc=s_pe)
        run("tensor", lambda e: e.matmul(ps[1][0:16, :], ones16f[0:1, 0:16], rsb_sb, start=True, stop=True), inc=s_pe)
        run("tensor", lambda e: e.matmul(ps[2][0:16, :], ones16f[0:1, 0:16], rsc_sb, start=True, stop=True), inc=s_pe)
        pe_rep = s_pe.n
        wait("vector", s_pe, pe_rep)
        for rb in range(2):
            run("vector", lambda e, rb=rb: e.tensor_tensor(
                out=dtn_sb[:, rb, :], in0=dt_sum[:, rb, :], in1=ps[0][:], op=ALU.mult), inc=s_dv)
        run("vector", lambda e: e.scalar_tensor_tensor(
            out=bu_sb[:], in0=b_sum[:], scalar=blw_sb[:, 0:1], in1=ps[1][0:16, :],
            op0=ALU.mult, op1=ALU.mult), inc=s_dv)
        run("vector", lambda e: e.scalar_tensor_tensor(
            out=cu_sb[:], in0=c_sum[:], scalar=blw_sb[:, 1:2], in1=ps[2][0:16, :],
            op0=ALU.mult, op1=ALU.mult), inc=s_dv)
        v_norm = s_dv.n
        for bk in (0, 1, 2):
            bank_guard[bk] = (s_dv, v_norm)

        if c == 0:
            for t, buf in (("dtn", dt_sum), ("bu", b_sum), ("cu", c_sum)):
                if t in taps and t not in tap_done:
                    tap_done.add(t)
                    wait("sync", s_dv, v_norm)
                    run("sync", lambda e, t=t, buf=buf: e.dma_start(
                        out=tap_aps[t], in_=buf[:]), inc=s_st, amt=16)

        wait("scalar", s_pe, pe_win_done_prev)
        for n in range(NST):
            pe_use_bank(3)
            wait("tensor", s_dv, v_norm)
            run("tensor", lambda e, n=n: e.matmul(
                ps[3][:], selmat[:, n, :], bu_sb[:], start=True, stop=True), inc=s_pe)
            pb = s_pe.n
            wait("scalar", s_pe, pb)
            ev = run("scalar", lambda e, n=n: e.activation(
                out=brep_sb[:, n, :], in_=ps[3][:], func=AF.Copy), inc=s_ac)
            bank_guard[3] = (s_ac, ev)
            pe_use_bank(3)
            run("tensor", lambda e, n=n: e.matmul(
                ps[3][:], selmat[:, n, :], cu_sb[:], start=True, stop=True), inc=s_pe)
            pcc = s_pe.n
            wait("scalar", s_pe, pcc)
            ev = run("scalar", lambda e, n=n: e.activation(
                out=crep_sb[:, n, :], in_=ps[3][:], func=AF.Copy), inc=s_ac)
            bank_guard[3] = (s_ac, ev)
        ac_rep = s_ac.n

        wait("vector", s_dv, v_hsave)
        prev_dadd = 0
        for db in range(NDB):
            bank = 4 + (db % 2)
            pe_use_bank(bank)
            wait("tensor", s_dv, v_norm)
            for rb in range(2):
                run("tensor", lambda e, db=db, rb=rb, bank=bank: e.matmul(
                    ps[bank][:], wdt_sb[:, rb, 128 * db:128 * (db + 1)],
                    dtn_sb[:, rb, :], start=(rb == 0), stop=(rb == 1)),
                    inc=(s_pe if rb == 1 else None))
            pd = s_pe.n
            wait("scalar", s_pe, pd)
            if prev_dadd:
                wait("scalar", s_dv, prev_dadd)
            run("scalar", lambda e, db=db, bank=bank: e.activation(
                out=relu_sb[:], in_=ps[bank][:], func=AF.Relu,
                bias=dtbias_sb[:, db:db + 1]), inc=s_ac)
            ev = run("scalar", lambda e, db=db, bank=bank: e.activation(
                out=abs_sb[:, 0, :], in_=ps[bank][:], func=AF.Abs,
                bias=dtbias_sb[:, db:db + 1]), inc=s_ac)
            bank_guard[bank] = (s_ac, ev)
            a1 = s_ac.n
            wait("scalar", s_ac, a1)
            run("scalar", lambda e: e.activation(
                out=abs_sb[:, 1, :], in_=abs_sb[:, 0, :], func=AF.Exp, scale=cst_sb[:, 5:6]), inc=s_ac)
            a2 = s_ac.n
            wait("scalar", s_ac, a2)
            run("scalar", lambda e: e.activation(
                out=abs_sb[:, 0, :], in_=abs_sb[:, 1, :], func=AF.Ln, bias=cst_sb[:, 4:5]), inc=s_ac)
            a3 = s_ac.n
            wait("vector", s_ac, a3)
            run("vector", lambda e, db=db: e.tensor_tensor(
                out=ud_sb[:, db, 0:TC], in0=relu_sb[:], in1=abs_sb[:, 0, :],
                op=ALU.add), inc=s_dv)
            prev_dadd = s_dv.n
        v_vready = s_dv.n

        if c == 0 and "delta" in taps and "delta" not in tap_done:
            tap_done.add("delta")
            wait("sync", s_dv, v_vready)
            run("sync", lambda e: e.dma_start(
                out=tap_aps["delta"], in_=ud_sb[:, :, 0:TC]), inc=s_st, amt=16)

        for db in range(NDB):
            ybank = 6 + (db % 2)
            pe_use_bank(ybank)
            wait("tensor", s_ac, ac_ub)
            run("tensor", lambda e, db=db, ybank=ybank: e.matmul(
                ps[ybank][:], dpdiag[:, db, :], ub_sb[:, db, :],
                start=True, stop=False), inc=s_pe)
            pb = db % 2
            wait("vector", s_dv, vdb_free[pb])
            wait("vector", s_dv, v_vready)
            run("vector", lambda e, db=db, pb=pb: e.tensor_tensor(
                out=v_db[:, pb, :], in0=ud_sb[:, db, 0:TC], in1=ub_sb[:, db, :],
                op=ALU.mult), inc=s_dv)
            v_vdb = s_dv.n
            for q in range(NGRP):
                it = (c * NDB + db) * NGRP + q
                qb = it % 2
                n0 = NQ * q
                sem_f, val_f = stage_free[qb]
                wait("scalar", sem_f, val_f)
                wait("scalar", s_dv, v_vready)
                wait("scalar", s_ac, ac_rep)
                for j in range(NQ):
                    run("scalar", lambda e, db=db, j=j, qb=qb, n0=n0: e.activation(
                        out=dax_sb[:, qb, j, 2:2 + TC], in_=ud_sb[:, db, 0:TC],
                        func=AF.Exp, scale=A_sb[:, db, n0 + j:n0 + j + 1]), inc=s_ac)
                ac_dax = s_ac.n
                wait("vector", sem_f, val_f)
                wait("vector", s_ac, ac_rep)
                wait("vector", s_dv, v_vdb)
                v_bc, b_bc = broadcast_tensor_aps(
                    v_db[:, pb, :][:, None, :], brep_sb[:, n0:n0 + NQ, :])
                run("vector", lambda e, qb=qb, v_bc=v_bc, b_bc=b_bc: e.tensor_tensor(
                    out=vbs_sb[:, qb, :, 2:2 + TC], in0=v_bc, in1=b_bc,
                    op=ALU.mult), inc=s_dv)
                run("vector", lambda e, db=db, qb=qb, n0=n0: e.tensor_copy(
                    out=vbs_sb[:, qb, :, 0:1],
                    in_=carry[:, db, n0:n0 + NQ][:, :, None]), inc=s_dv)
                v_vb = s_dv.n
                wait("vector", s_ac, ac_dax)
                wait("vector", s_dv, v_vb)
                run("vector", lambda e, qb=qb: e.tensor_tensor_scan(
                    out=ss_sb[:, qb, :, :].rearrange("p a s -> p (a s)"),
                    data0=dax_sb[:, qb, :, :].rearrange("p a s -> p (a s)"),
                    data1=vbs_sb[:, qb, :, :].rearrange("p a s -> p (a s)"),
                    initial=0.0, op0=ALU.mult, op1=ALU.add), inc=s_dv)
                v_scan = s_dv.n
                wait("vector", s_dv, v_scan)
                run("vector", lambda e, db=db, qb=qb, n0=n0: e.tensor_copy(
                    out=carry[:, db, n0:n0 + NQ],
                    in_=ss_sb[:, qb, :, SEG - 1:SEG].rearrange("p a o -> p (a o)")), inc=s_dv)
                sem_s, val_s = sc_free[qb]
                wait("gpsimd", sem_s, val_s)
                wait("gpsimd", s_dv, v_scan)
                wait("gpsimd", s_ac, ac_rep)
                run("gpsimd", lambda e, qb=qb, n0=n0: e.tensor_tensor(
                    out=sc_sb[:, qb, :, :], in0=ss_sb[:, qb, :, 2:2 + TC],
                    in1=crep_sb[:, n0:n0 + NQ, :], op=ALU.mult), inc=s_gp)
                gp_sc = s_gp.n
                stage_free[qb] = (s_gp, gp_sc)
                wait("tensor", s_gp, gp_sc)
                last = (q == NGRP - 1)
                for j in range(NQ):
                    run("tensor", lambda e, qb=qb, j=j, ybank=ybank,
                        st=(last and j == NQ - 1): e.matmul(
                        ps[ybank][:], ident[:], sc_sb[:, qb, j, :],
                        start=False, stop=st),
                        inc=(s_pe if (last and j == NQ - 1) else None))
                if not last:
                    run("tensor", lambda e: e.nop(), inc=s_pe)
                sc_free[qb] = (s_pe, s_pe.n)
            pe_y = s_pe.n
            vdb_free[pb] = s_dv.n
            wait("vector", s_pe, pe_y)
            ev = run("vector", lambda e, db=db, ybank=ybank: e.tensor_tensor(
                out=yg_sb[:, db, :], in0=ps[ybank][:], in1=sg_sb[:, db, :],
                op=ALU.mult), inc=s_dv)
            bank_guard[ybank] = (s_dv, ev)
        v_yg = s_dv.n
        v_yg_prev = v_yg
        gp_cmul_prev = s_gp.n

        if c == 0:
            for t, buf, gate in (("yg", yg_sb, (s_dv, v_yg)),
                                 ("carry", carry, (s_dv, v_yg))):
                if t in taps and t not in tap_done:
                    tap_done.add(t)
                    wait("sync", gate[0], gate[1])
                    run("sync", lambda e, t=t, buf=buf: e.dma_start(
                        out=tap_aps[t], in_=buf[:]), inc=s_st, amt=16)

        if c == 0 and "slab0" in taps and "slab0" not in tap_done:
            tap_done.add("slab0")
            wait("sync", s_wout, wout_vals[0])
            run("sync", lambda e: e.dma_start(out=tap_aps["slab0"], in_=wout_slab[:, 0, :, :]), inc=s_st, amt=16)
        if c == 0 and "yg2" in taps and "yg2" not in tap_done:
            tap_done.add("yg2")
            wait("sync", s_dv, v_yg)
            run("sync", lambda e: e.dma_start(out=tap_aps["yg2"], in_=yg_sb[:]), inc=s_st, amt=16)
        att_store = []
        for hb in range(32):
            bank = hb % 2
            pe_use_bank(bank)
            wait("tensor", s_dv, v_yg)
            for half in range(2):
                k = 64 * c + 2 * hb + half
                buf = k % 2
                wait("tensor", s_wout, wout_vals[k])
                for dbh in range(8):
                    db = 8 * half + dbh
                    run("tensor", lambda e, dbh=dbh, db=db, buf=buf, bank=bank, half=half: e.matmul(
                        ps[bank][:], wout_slab[:, buf, dbh, :], yg_sb[:, db, :],
                        start=(half == 0 and dbh == 0), stop=(half == 1 and dbh == 7)),
                        inc=(s_pe if dbh == 7 else None))
                half_done = s_pe.n
                nxt = k + 2
                if nxt < 64 * n_chunks:
                    nhb, nhalf = (nxt % 64) // 2, nxt % 2
                    wait("sync", s_pe, half_done)
                    wout_vals.append(run("sync", load_wout_half(nhb, nhalf, buf),
                                         inc=s_wout, amt=16))
            pe_o = s_pe.n
            wait("scalar", s_pe, pe_o)
            if c == 0 and hb == 0 and "psout" in taps and "psout" not in tap_done:
                tap_done.add("psout")
                run("scalar", lambda e, bank=bank: e.activation(
                    out=acc_sb[:], in_=ps[bank][:], func=AF.Copy), inc=s_ac)
                wait("sync", s_ac, s_ac.n)
                run("sync", lambda e: e.dma_start(out=tap_aps["psout"], in_=acc_sb[:]), inc=s_st, amt=16)
                em("scalar", lambda e, s=s_st.h, vv=s_st.n: e.wait_ge(s, vv))
            if hb >= 2:
                wait("scalar", s_shtl, att_store[hb - 2])
            ev = run("scalar", lambda e, hb=hb, bank=bank: e.activation(
                out=attn_st[:, hb % 2, :], in_=ps[bank][:], func=AF.Copy), inc=s_ac)
            bank_guard[bank] = (s_ac, ev)
            wait("sync", s_ac, s_ac.n)
            att_store.append(run("sync", lambda e, c=c, hb=hb: e.dma_start(
                out=attn_in[c].rearrange("(qq hb2 p) t -> p hb2 qq t", p=128, qq=4)[:, hb, :, :],
                in_=attn_st[:, hb % 2, :].rearrange("p (qq t) -> p qq t", qq=4)),
                inc=s_shtl, amt=16))
        wait("gpsimd", s_shtl, s_shtl.n)
        run("gpsimd", lambda e, c=c: e.collective_compute(
            "ReduceScatter", ALU.add, replica_groups=GROUPS,
            ins=[attn_in[c].opt()], outs=[attn_rs[c].opt()]), inc=s_cc)
    cc_all = s_cc.n
    if "woutbf" in taps:
        wait("sync", s_wprep, 32)
        run("sync", lambda e: e.dma_start(out=tap_aps["woutbf"], in_=w_outT_bf[:, 0:128]), inc=s_st, amt=16)
    if "attnp" in taps:
        wait("sync", s_cc, cc_all)
        run("sync", lambda e: e.dma_start(out=tap_aps["attnp"], in_=attn_in[0]), inc=s_st, amt=16)
    if "attnq" in taps:
        wait("sync", s_cc, cc_all)
        run("sync", lambda e: e.dma_start(out=tap_aps["attnq"], in_=attn_rs[0]), inc=s_st, amt=16)
    esA.close()

    with ExitStack() as es2:
        sb2 = lambda name, shape, dt: es2.enter_context(nc.sbuf_tensor(name, list(shape), dt))
        at_sb = sb2("at_sb", [128, 32, 2, 128], F32)
        xm_sb = sb2("xm_sb", [128, 2, H], F32)
        fz_sb = sb2("fz_sb", [128, 2, H], F32)
        st2_sb = sb2("st2_sb", [128, 8], F32)
        ln2_sb = sb2("ln2_sb", [128, 2, H], BF16)
        ln2T_sb = sb2("ln2T_sb", [128, 32, 256], BF16)
        wr_sb = sb2("wr_sb", [128, 32, 64], BF16)
        scf_sb = sb2("scf_sb", [64, 256], F32)
        sctm_sb = sb2("sctm_sb", [128, 2, 64], F32)
        mx_sb = sb2("mx_sb", [128, 2, 8], F32)
        mx2_sb = sb2("mx2_sb", [128, 2, 8], F32)
        scr2_sb = sb2("scr2_sb", [128, 64], F32)
        msk_sb = sb2("msk_sb", [128, 64], F32)
        gt_sb = sb2("gt_sb", [128, 2, 64], BF16)
        gT_sb = sb2("gT_sb", [64, 2, 128], BF16)
        grep_sb = sb2("grep_sb", [128, 8, 256], BF16)
        hg_sb = sb2("hg_sb", [128, 8, 256], BF16)
        asl_sb = sb2("asl_sb", [128, 2, 32, 128], BF16)
        bsl_sb = sb2("bsl_sb", [128, 2, 8, 128], BF16)
        ffn_f = sb2("ffn_f", [128, 4, 256], F32)

        wait("sync", s_cc, cc_all)
        run("sync", lambda e: e.dma_start(
            out=wr_sb[:], in_=ffn_wrT_bf.rearrange("(hb p) k -> p hb k", p=128)),
            inc=s_dma, amt=16)
        dma_wr = s_dma.n

        n_blocks = (n_chunks + 1) // 2
        asl_vals = []
        bsl_vals = []

        def load_asl(krb, buf):
            def f(e):
                return e.dma_start(
                    out=asl_sb[:, buf, :, :],
                    in_=ffn_aT_bf[:, 128 * krb:128 * (krb + 1)].rearrange(
                        "(hb p) k -> p hb k", p=128))
            return f

        def load_bsl(hb, buf):
            def f(e):
                return e.dma_start(
                    out=bsl_sb[:, buf, :, :],
                    in_=ffn_bT_bf[:, 128 * hb:128 * (hb + 1)].rearrange(
                        "(krb p) h -> p krb h", p=128))
            return f

        asl_vals.append(run("sync", load_asl(0, 0), inc=s_win, amt=16))
        bsl_vals.append(run("sync", load_bsl(0, 0), inc=s_wout, amt=16))

        for bk in range(n_blocks):
            cids = [2 * bk + i for i in range(2) if 2 * bk + i < n_chunks]
            ncid = len(cids)
            tw = 128 * ncid
            wait("sync", s_cc, cc_all)
            if bk > 0:
                wait("sync", s_dv, v_out_prev)
                wait("sync", s_pe, s_pe.n)
            for ci, c in enumerate(cids):
                run("sync", lambda e, ci=ci, c=c: e.dma_start(
                    out=at_sb[:, :, ci, :],
                    in_=attn_rs[c].rearrange("(hb p) t -> p hb t", p=128)),
                    inc=s_dma, amt=16)
                run("sync", lambda e, ci=ci, c=c: e.dma_start(
                    out=xm_sb[:, ci, :], in_=x_own[128 * c:128 * (c + 1), :]),
                    inc=s_dma, amt=16)
            dma_at = s_dma.n
            wait("tensor", s_dma, dma_at)
            for ci in range(ncid):
                for hg in range(8):
                    bank = hg % 2
                    pe_use_bank(bank)
                    for k in range(4):
                        hb = 4 * hg + k
                        run("tensor", lambda e, ci=ci, hb=hb, k=k, bank=bank: e.transpose(
                            ps[bank][:, 128 * k:128 * (k + 1)],
                            at_sb[:, hb, ci, :], identf[:]),
                            inc=(s_pe if k == 3 else None))
                    pt = s_pe.n
                    wait("scalar", s_pe, pt)
                    ev = run("scalar", lambda e, ci=ci, hg=hg, bank=bank: e.activation(
                        out=fz_sb[:, ci, 512 * hg:512 * (hg + 1)], in_=ps[bank][:],
                        func=AF.Copy), inc=s_ac)
                    bank_guard[bank] = (s_ac, ev)
            ac_att = s_ac.n
            wait("vector", s_ac, ac_att)
            wait("vector", s_dma, dma_at)
            for ci in range(ncid):
                run("vector", lambda e, ci=ci: e.tensor_tensor(
                    out=xm_sb[:, ci, :], in0=xm_sb[:, ci, :], in1=fz_sb[:, ci, :],
                    op=ALU.add), inc=s_dv)
                vx = s_dv.n
                wait("vector", s_dv, vx)
                run("vector", lambda e, ci=ci: e.tensor_reduce(
                    out=st2_sb[:, 0:1], in_=xm_sb[:, ci, :], axis=AX.X, op=ALU.add), inc=s_dv)
                wait("scalar", s_dv, vx)
                run("scalar", lambda e, ci=ci: e.activation(
                    out=fz_sb[:, ci, :], in_=xm_sb[:, ci, :], func=AF.Square), inc=s_ac)
                va = s_ac.n
                wait("vector", s_ac, va)
                run("vector", lambda e, ci=ci: e.tensor_reduce(
                    out=st2_sb[:, 1:2], in_=fz_sb[:, ci, :], axis=AX.X, op=ALU.add), inc=s_dv)
                vr = s_dv.n
                wait("vector", s_dv, vr)
                run("vector", lambda e: e.tensor_scalar(
                    out=st2_sb[:, 2:3], in0=st2_sb[:, 0:1], scalar1=1.0 / H, scalar2=None, op0=ALU.mult), inc=s_dv)
                run("vector", lambda e: e.tensor_scalar(
                    out=st2_sb[:, 4:5], in0=st2_sb[:, 1:2], scalar1=1.0 / H, scalar2=None, op0=ALU.mult), inc=s_dv)
                vr2 = s_dv.n
                wait("vector", s_dv, vr2)
                run("vector", lambda e: e.tensor_tensor(
                    out=st2_sb[:, 3:4], in0=st2_sb[:, 2:3], in1=st2_sb[:, 2:3], op=ALU.mult), inc=s_dv)
                vr3 = s_dv.n
                wait("vector", s_dv, vr3)
                run("vector", lambda e: e.tensor_tensor(
                    out=st2_sb[:, 5:6], in0=st2_sb[:, 4:5], in1=st2_sb[:, 3:4], op=ALU.subtract), inc=s_dv)
                vr4 = s_dv.n
                wait("scalar", s_dv, vr4)
                run("scalar", lambda e: e.activation(
                    out=st2_sb[:, 6:7], in_=st2_sb[:, 5:6], func=AF.Sqrt, bias=cst_sb[:, 0:1]), inc=s_ac)
                va2 = s_ac.n
                wait("vector", s_ac, va2)
                run("vector", lambda e: e.reciprocal(out=st2_sb[:, 7:8], in_=st2_sb[:, 6:7]), inc=s_dv)
                vr5 = s_dv.n
                wait("vector", s_dv, vr5)
                run("vector", lambda e, ci=ci: e.tensor_scalar(
                    out=ln2_sb[:, ci, :], in0=xm_sb[:, ci, :], scalar1=st2_sb[:, 2:3],
                    scalar2=st2_sb[:, 7:8], op0=ALU.subtract, op1=ALU.mult), inc=s_dv)
            v_ln2 = s_dv.n
            wait("tensor", s_dv, v_ln2)
            for j in range(32):
                bank = j % 2
                pe_use_bank(bank)
                for ci in range(ncid):
                    run("tensor", lambda e, ci=ci, j=j, bank=bank: e.transpose(
                        ps[bank][:].bitcast(BF16)[:, 128 * ci:128 * (ci + 1)],
                        ln2_sb[:, ci, 128 * j:128 * (j + 1)], ident[:]),
                        inc=(s_pe if ci == ncid - 1 else None))
                pj = s_pe.n
                wait("scalar", s_pe, pj)
                ev = run("scalar", lambda e, j=j, bank=bank, tw=tw: e.activation(
                    out=ln2T_sb[:, j, 0:tw],
                    in_=ps[bank][:].bitcast(BF16)[:, 0:tw], func=AF.Copy), inc=s_ac)
                bank_guard[bank] = (s_ac, ev)
            ac_lnT = s_ac.n
            pe_use_bank(2)
            wait("tensor", s_ac, ac_lnT)
            wait("tensor", s_dma, dma_wr)
            for hb in range(32):
                run("tensor", lambda e, hb=hb, tw=tw: e.matmul(
                    ps[2][0:64, 0:tw], wr_sb[:, hb, :], ln2T_sb[:, hb, 0:tw],
                    start=(hb == 0), stop=(hb == 31)), inc=(s_pe if hb == 31 else None))
            p_sc = s_pe.n
            wait("scalar", s_pe, p_sc)
            ev = run("scalar", lambda e, tw=tw: e.activation(
                out=scf_sb[:, 0:tw], in_=ps[2][0:64, 0:tw], func=AF.Copy), inc=s_ac)
            bank_guard[2] = (s_ac, ev)
            ac_sc = s_ac.n
            pe_use_bank(3)
            wait("tensor", s_ac, ac_sc)
            for ci in range(ncid):
                run("tensor", lambda e, ci=ci: e.transpose(
                    ps[3][:, 64 * ci:64 * (ci + 1)],
                    scf_sb[0:64, 128 * ci:128 * (ci + 1)], identf[0:64, 0:64]), inc=s_pe)
            p_st = s_pe.n
            wait("scalar", s_pe, p_st)
            ev = run("scalar", lambda e, ncid=ncid: e.activation(
                out=sctm_sb[:].rearrange("p a k -> p (a k)")[:, 0:64 * ncid],
                in_=ps[3][:, 0:64 * ncid], func=AF.Copy), inc=s_ac)
            bank_guard[3] = (s_ac, ev)
            ac_stm = s_ac.n
            for ci in range(ncid):
                wait("vector", s_ac, ac_stm)
                run("vector", lambda e, ci=ci: e.max(
                    out=mx_sb[:, ci, :], in_=sctm_sb[:, ci, :]), inc=s_dv)
                v_m1 = s_dv.n
                wait("vector", s_dv, v_m1)
                run("vector", lambda e, ci=ci: e.match_replace(
                    out=scr2_sb[:], in_to_replace=mx_sb[:, ci, :],
                    in_values=sctm_sb[:, ci, :], imm_value=-1e30), inc=s_dv)
                v_m2 = s_dv.n
                wait("vector", s_dv, v_m2)
                run("vector", lambda e, ci=ci: e.max(
                    out=mx2_sb[:, ci, :], in_=scr2_sb[:]), inc=s_dv)
                v_m3 = s_dv.n
                wait("vector", s_dv, v_m3)
                run("vector", lambda e, ci=ci: e.tensor_single_scalar(
                    out=msk_sb[:], in_=sctm_sb[:, ci, :],
                    scalar=mx2_sb[:, ci, 7:8], op=ALU.is_ge), inc=s_dv)
                run("vector", lambda e, ci=ci: e.tensor_scalar(
                    out=st2_sb[:, 0:1], in0=mx_sb[:, ci, 0:1], scalar1=-1.0,
                    scalar2=None, op0=ALU.mult), inc=s_dv)
                v_m4 = s_dv.n
                wait("scalar", s_dv, v_m4)
                run("scalar", lambda e, ci=ci: e.activation(
                    out=scr2_sb[:], in_=sctm_sb[:, ci, :], func=AF.Exp,
                    bias=st2_sb[:, 0:1]), inc=s_ac)
                va3 = s_ac.n
                wait("vector", s_ac, va3)
                run("vector", lambda e: e.tensor_tensor(
                    out=scr2_sb[:], in0=scr2_sb[:], in1=msk_sb[:], op=ALU.mult), inc=s_dv)
                v_m5 = s_dv.n
                wait("vector", s_dv, v_m5)
                run("vector", lambda e: e.tensor_reduce(
                    out=st2_sb[:, 1:2], in_=scr2_sb[:], axis=AX.X, op=ALU.add), inc=s_dv)
                v_m6 = s_dv.n
                wait("vector", s_dv, v_m6)
                run("vector", lambda e: e.reciprocal(out=st2_sb[:, 2:3], in_=st2_sb[:, 1:2]), inc=s_dv)
                v_m7 = s_dv.n
                wait("vector", s_dv, v_m7)
                run("vector", lambda e: e.tensor_scalar(
                    out=st2_sb[:, 3:4], in0=st2_sb[:, 2:3], scalar1=2.0,
                    scalar2=None, op0=ALU.mult), inc=s_dv)
                v_m8 = s_dv.n
                wait("vector", s_dv, v_m8)
                run("vector", lambda e, ci=ci: e.tensor_scalar(
                    out=gt_sb[:, ci, :], in0=scr2_sb[:], scalar1=st2_sb[:, 3:4],
                    scalar2=None, op0=ALU.mult), inc=s_dv)
            v_gt = s_dv.n
            if bk == 0 and "sct" in taps and "sct" not in tap_done:
                tap_done.add("sct")
                wait("sync", s_dv, v_gt)
                run("sync", lambda e: e.dma_start(out=tap_aps["sct"], in_=sctm_sb[:]), inc=s_st, amt=16)
                run("sync", lambda e: e.dma_start(out=tap_aps["gts"], in_=gt_sb[:]), inc=s_st, amt=16)
                em("vector", lambda e, s=s_st.h, vv=s_st.n: e.wait_ge(s, vv))
            pe_use_bank(3)
            wait("tensor", s_dv, v_gt)
            for ci in range(ncid):
                run("tensor", lambda e, ci=ci: e.transpose(
                    ps[3][:].bitcast(BF16)[0:64, 128 * ci:128 * (ci + 1)],
                    gt_sb[:, ci, :], ident[:]), inc=s_pe)
            p_gt = s_pe.n
            wait("scalar", s_pe, p_gt)
            ev = run("scalar", lambda e, tw=tw: e.activation(
                out=gT_sb[:].rearrange("k a t -> k (a t)")[:, 0:tw],
                in_=ps[3][:].bitcast(BF16)[0:64, 0:tw], func=AF.Copy), inc=s_ac)
            bank_guard[3] = (s_ac, ev)
            ac_gT = s_ac.n
            for kb in range(8):
                bank = kb % 2
                pe_use_bank(bank)
                wait("tensor", s_ac, ac_gT)
                run("tensor", lambda e, kb=kb, tw=tw: e.matmul(
                    ps[kb % 2][:, 0:tw], p8mats[:, kb, :],
                    gT_sb[:].rearrange("k a t -> k (a t)")[:, 0:tw],
                    start=True, stop=True), inc=s_pe)
                pk = s_pe.n
                wait("scalar", s_pe, pk)
                ev = run("scalar", lambda e, kb=kb, bank=bank, tw=tw: e.activation(
                    out=grep_sb[:, kb, 0:tw], in_=ps[bank][:, 0:tw],
                    func=AF.Copy), inc=s_ac)
                bank_guard[bank] = (s_ac, ev)
            ac_grep = s_ac.n
            for krb in range(8):
                bank = 2 + (krb % 2)
                pe_use_bank(bank)
                kk = 8 * bk + krb
                buf = kk % 2
                wait("tensor", s_win, asl_vals[kk])
                for hb in range(32):
                    run("tensor", lambda e, hb=hb, buf=buf, bank=bank, tw=tw: e.matmul(
                        ps[bank][:, 0:tw], asl_sb[:, buf, hb, :],
                        ln2T_sb[:, hb, 0:tw], start=(hb == 0), stop=(hb == 31)),
                        inc=(s_pe if hb == 31 else None))
                ph = s_pe.n
                nxtk = kk + 1
                if nxtk < 8 * n_blocks:
                    wait("sync", s_pe, ph)
                    asl_vals.append(run("sync", load_asl(nxtk % 8, nxtk % 2),
                                        inc=s_win, amt=16))
                wait("vector", s_pe, ph)
                wait("vector", s_ac, ac_grep)
                ev = run("vector", lambda e, krb=krb, bank=bank, tw=tw: e.tensor_tensor(
                    out=hg_sb[:, krb, 0:tw], in0=ps[bank][:, 0:tw],
                    in1=grep_sb[:, krb, 0:tw], op=ALU.mult), inc=s_dv)
                bank_guard[bank] = (s_dv, ev)
            v_hg = s_dv.n
            if bk == 0 and "hg" in taps and "hg" not in tap_done:
                tap_done.add("hg")
                wait("sync", s_dv, v_hg)
                run("sync", lambda e: e.dma_start(out=tap_aps["hg"], in_=hg_sb[:]), inc=s_st, amt=16)
                run("sync", lambda e: e.dma_start(out=tap_aps["grep"], in_=grep_sb[:]), inc=s_st, amt=16)
            for hb in range(32):
                bank = 4 + (hb % 2)
                pe_use_bank(bank)
                kk = 32 * bk + hb
                buf = kk % 2
                wait("tensor", s_wout, bsl_vals[kk])
                wait("tensor", s_dv, v_hg)
                for krb in range(8):
                    run("tensor", lambda e, krb=krb, buf=buf, bank=bank, tw=tw: e.matmul(
                        ps[bank][:, 0:tw], bsl_sb[:, buf, krb, :],
                        hg_sb[:, krb, 0:tw], start=(krb == 0), stop=(krb == 7)),
                        inc=(s_pe if krb == 7 else None))
                pf = s_pe.n
                nxtk = kk + 1
                if nxtk < 32 * n_blocks:
                    wait("sync", s_pe, pf)
                    bsl_vals.append(run("sync", load_bsl(nxtk % 32, nxtk % 2),
                                        inc=s_wout, amt=16))
                wait("scalar", s_pe, pf)
                ev = run("scalar", lambda e, hb=hb, bank=bank, tw=tw: e.activation(
                    out=ffn_f[:, hb % 4, 0:tw], in_=ps[bank][:, 0:tw],
                    func=AF.Copy), inc=s_ac)
                bank_guard[bank] = (s_ac, ev)
                if hb % 4 == 3:
                    hb0 = hb - 3
                    ac_f = s_ac.n
                    for ci in range(ncid):
                        bank2 = 6 + (ci % 2)
                        pe_use_bank(bank2)
                        wait("tensor", s_ac, ac_f)
                        for k in range(4):
                            run("tensor", lambda e, k=k, ci=ci, bank2=bank2: e.transpose(
                                ps[bank2][:, 128 * k:128 * (k + 1)],
                                ffn_f[:, k, 128 * ci:128 * (ci + 1)], identf[:]),
                                inc=(s_pe if k == 3 else None))
                        pt2 = s_pe.n
                        wait("scalar", s_pe, pt2)
                        ev = run("scalar", lambda e, ci=ci, hb0=hb0, bank2=bank2: e.activation(
                            out=fz_sb[:, ci, 128 * hb0:128 * (hb0 + 4)],
                            in_=ps[bank2][:], func=AF.Copy), inc=s_ac)
                        bank_guard[bank2] = (s_ac, ev)
            ac_ffn = s_ac.n
            wait("vector", s_ac, ac_ffn)
            for ci in range(ncid):
                run("vector", lambda e, ci=ci: e.tensor_tensor(
                    out=xm_sb[:, ci, :], in0=xm_sb[:, ci, :], in1=fz_sb[:, ci, :],
                    op=ALU.add), inc=s_dv)
            v_out = s_dv.n
            v_out_prev = v_out
            wait("sync", s_dv, v_out)
            for ci, c in enumerate(cids):
                run("sync", lambda e, ci=ci, c=c: e.dma_start(
                    out=y_out[128 * c:128 * (c + 1), :], in_=xm_sb[:, ci, :]),
                    inc=s_st, amt=16)

    wait("sync", s_st, s_st.n)

    with nc.Block() as block:
        @block.sync
        def _(eng):
            for f in plan["sync"]:
                f(eng)

        @block.vector
        def _(eng):
            for f in plan["vector"]:
                f(eng)

        @block.scalar
        def _(eng):
            for f in plan["scalar"]:
                f(eng)

        @block.tensor
        def _(eng):
            for f in plan["tensor"]:
                f(eng)

        @block.gpsimd
        def _(eng):
            for f in plan["gpsimd"]:
                f(eng)

    es.close()
    return nc


def make_in_maps(inputs, n_chunks=L // TC):
    x = np.asarray(inputs["L0_attn_hidden_states"])
    W_in = np.asarray(inputs["L0_attn_W_in"])
    conv_w_np = np.asarray(inputs["L0_attn_conv_w"])
    conv_b_np = np.asarray(inputs["L0_attn_conv_b"])
    Wx = np.asarray(inputs["L0_attn_Wx"])
    dt_ln_w_np = np.asarray(inputs["L0_attn_dt_ln_w"])
    b_ln_w_np = np.asarray(inputs["L0_attn_b_ln_w"])
    c_ln_w_np = np.asarray(inputs["L0_attn_c_ln_w"])
    Wdt = np.asarray(inputs["L0_attn_Wdt"])
    dt_bias_np = np.asarray(inputs["L0_attn_dt_bias"])
    A = np.asarray(inputs["L0_attn_A"])
    D_param = np.asarray(inputs["L0_attn_D_param"])
    W_out = np.asarray(inputs["L0_attn_W_out"])
    fA = np.asarray(inputs["L0_ffn_A"])
    fB = np.asarray(inputs["L0_ffn_B"])
    fWr = np.asarray(inputs["L0_ffn_Wr"])

    C = np.ascontiguousarray
    in_maps = []
    for core in range(NCORES):
        b, r = core // 4, core % 4
        dsl = slice(DC * r, DC * (r + 1))
        rows = []
        for c in range(n_chunks):
            t0 = TC * c + 128 * r
            rows.append(x[b, t0:t0 + 128, :])
        x_own_np = C(np.concatenate(rows, 0).astype(np.float32))
        w_in_sl = np.concatenate([W_in[dsl], W_in[D + DC * r: D + DC * (r + 1)]], 0)
        def pmaj(v, nb):
            return C(np.ascontiguousarray(np.swapaxes(v.reshape(nb, 128, *v.shape[1:]), 0, 1)).reshape(v.shape))
        m = {
            "x_own": x_own_np,
            "w_inT": C(w_in_sl.T),
            "w_outT": C(W_out[:, dsl].T),
            "wxT": pmaj(C(Wx[:, dsl].T), NDB),
            "wdtT": pmaj(C(Wdt[dsl].T), 2),
            "conv_w": pmaj(C(conv_w_np[dsl]), NDB),
            "conv_b": pmaj(C(conv_b_np[dsl]), NDB),
            "a_par": pmaj(C(A[dsl]), NDB),
            "d_par": pmaj(C(D_param[dsl]), NDB),
            "dt_bias": pmaj(C(dt_bias_np[dsl]), NDB),
            "dt_ln_w": pmaj(C(dt_ln_w_np), 2),
            "b_ln_w": C(b_ln_w_np),
            "c_ln_w": C(c_ln_w_np),
            "ffn_aT": C(fA.reshape(1024, H).T),
            "ffn_bT": C(fB.transpose(0, 2, 1).reshape(1024, H)),
            "ffn_wrT": C(fWr.T),
        }
        in_maps.append(m)
    return in_maps


def assemble_out(results, n_chunks=L // TC):
    out = np.zeros((BATCH, L, H), np.float32)
    for core in range(NCORES):
        b, r = core // 4, core % 4
        yo = results[core]["y_out"]
        for c in range(n_chunks):
            t0 = TC * c + 128 * r
            out[b, t0:t0 + 128, :] = yo[128 * c:128 * (c + 1), :]
    return out


def kernel(**inputs):
    n_chunks = L // TC
    nc = build_program(n_chunks=n_chunks)
    in_maps = make_in_maps(inputs, n_chunks)
    res = run_bass_kernel_spmd(nc, in_maps, core_ids=list(range(NCORES)))
    return assemble_out(res.results, n_chunks)


# version 02
# speedup vs baseline: 1.0323x; 1.0323x over previous
import numpy as np
import ml_dtypes
from contextlib import ExitStack

import concourse.bass as bass
import concourse.mybir as mybir
from concourse.bass_utils import run_bass_kernel_spmd
from concourse.bass import broadcast_tensor_aps

F32 = mybir.dt.float32
BF16 = mybir.dt.bfloat16
I32 = mybir.dt.int32
AF = mybir.ActivationFunctionType
ALU = mybir.AluOpType
AX = mybir.AxisListType
BF = ml_dtypes.bfloat16

H = 4096
D = 8192
NST = 16
RDT = 256
BATCH = 2
L = 8192
DC = 2048
NDB = DC // 128
TOWN = 2048
TC = 512
SEG = TC + 2
NQ = 2
NGRP = NST // NQ
NCORES = 8
GROUPS = [[0, 1, 2, 3], [4, 5, 6, 7]]
ENGINES = ("sync", "vector", "scalar", "tensor", "gpsimd")


def build_program(n_chunks=L // TC, taps=(), only_stage1=False, skip_ag=False, lean_init=0):
    nc = bass.Bass("TRN2", target_bir_lowering=False)
    es = ExitStack()
    E = es.enter_context

    def inp(name, shape):
        return nc.dram_tensor(name, list(shape), F32, kind="ExternalInput").ap()

    TOWNE = 128 * n_chunks
    x_own = inp("x_own", (TOWNE, H))
    w_inT = inp("w_inT", (8192, 2048))
    w_outT = inp("w_outT", (8192, 1024))
    wxT = inp("wxT", (DC, 288))
    wdtT = inp("wdtT", (RDT, DC))
    conv_w = inp("conv_w", (DC, 4))
    conv_b = inp("conv_b", (DC,))
    a_par = inp("a_par", (DC, NST))
    d_par = inp("d_par", (DC,))
    dt_bias = inp("dt_bias", (DC,))
    dt_ln_w = inp("dt_ln_w", (RDT,))
    b_ln_w = inp("b_ln_w", (NST,))
    c_ln_w = inp("c_ln_w", (NST,))
    ffn_aT = inp("ffn_aT", (1024, 4096))
    ffn_bT = inp("ffn_bT", (4096, 1024))
    ffn_wrT = inp("ffn_wrT", (128, 2048))

    y_out = nc.dram_tensor("y_out", [TOWNE, H], F32, kind="ExternalOutput").ap()

    w_inT_bf = nc.dram_tensor("w_inT_bf", [8192, 2048], BF16).ap()
    w_outT_bf = nc.dram_tensor("w_outT_bf", [8192, 1024], BF16).ap()
    ffn_aT_bf = nc.dram_tensor("ffn_aT_bf", [1024, 4096], BF16).ap()
    ffn_bT_bf = nc.dram_tensor("ffn_bT_bf", [4096, 1024], BF16).ap()
    ffn_wrT_bf = nc.dram_tensor("ffn_wrT_bf", [128, 2048], BF16).ap()
    ag_in = nc.dram_tensor("ag_in", [H, TOWNE], BF16).ap()
    ag_out = nc.dram_tensor("ag_out", [4 * H, TOWNE], BF16).ap()
    ssm_in = [nc.dram_tensor(f"ssm_in{c}", [288, TC], F32).ap() for c in range(n_chunks)]
    ssm_out = [nc.dram_tensor(f"ssm_out{c}", [288, TC], F32).ap() for c in range(n_chunks)]
    attn_in = [nc.dram_tensor(f"attn_in{c}", [4 * H, 128], F32).ap() for c in range(n_chunks)]
    attn_rs = [nc.dram_tensor(f"attn_rs{c}", [H, 128], F32).ap() for c in range(n_chunks)]

    tap_aps = {}
    tap_shapes = {
        "ub": ((128, NDB, TC), BF16), "delta": ((128, NDB, TC), F32),
        "yg": ((128, NDB, TC), BF16), "sg": ((128, NDB, TC), BF16),
        "xt": ((128, 32, TC), BF16), "carry": ((128, NDB, NST), F32),
        "dtn": ((128, 2, TC), F32), "bu": ((16, TC), F32), "cu": ((16, TC), F32),
        "ulin": ((128, NDB, TC + 3), F32),
        "attnq": ((H, 128), F32),
        "attnp": ((4 * H, 128), F32),
        "woutbf": ((DC, 128), BF16),
        "psout": ((128, TC), F32),
        "slab0": ((128, 8, 128), BF16),
        "sct": ((128, 2, 64), F32),
        "gts": ((128, 2, 64), BF16),
        "grep": ((128, 8, 256), BF16),
        "hg": ((128, 8, 256), BF16),
        "yg2": ((128, NDB, TC), BF16),
        "agin": ((H, min(TC, 128 * n_chunks)), BF16),
        "agout": ((H, min(TC, 128 * n_chunks)), BF16),
    }
    for t in taps:
        shp, dt = tap_shapes[t]
        tap_aps[t] = nc.dram_tensor("tap_" + t, list(shp), dt,
                                    kind="ExternalOutput").ap()

    plan = {e: [] for e in ENGINES}

    def em(engine, fn):
        plan[engine].append(fn)

    class Sem:
        __slots__ = ("h", "n", "name")

        def __init__(self, name):
            self.h = E(nc.semaphore(name))
            self.n = 0
            self.name = name

    last_wait = {e: {} for e in ENGINES}

    def wait(engine, sem, val=None):
        v = sem.n if val is None else val
        if v <= 0:
            return
        lw = last_wait[engine]
        if lw.get(sem.name, 0) >= v:
            return
        lw[sem.name] = v
        em(engine, lambda eng, s=sem.h, vv=v: eng.wait_ge(s, vv))

    def run(engine, fn, inc=None, amt=1):
        if inc is None:
            em(engine, fn)
            return None
        inc.n += amt
        em(engine, lambda eng, f=fn, s=inc.h, a=amt: f(eng).then_inc(s, a))
        return inc.n

    s_dma = Sem("s_dma")
    s_wprep = Sem("s_wprep")
    s_cc = Sem("s_cc")
    s_pe = Sem("s_pe")
    s_dv = Sem("s_dv")
    s_ac = Sem("s_ac")
    s_gp = Sem("s_gp")
    s_st = Sem("s_st")
    s_xt = Sem("s_xt")
    s_win = Sem("s_win")
    s_wout = Sem("s_wout")
    s_shtl = Sem("s_shtl")

    def sbt(name, shape, dt):
        return E(nc.sbuf_tensor(name, list(shape), dt))

    esA = ExitStack()

    def sbtA(name, shape, dt):
        return esA.enter_context(nc.sbuf_tensor(name, list(shape), dt))

    A_sb = sbt("A_sb", [128, NDB, NST], F32)
    cw_sb = sbt("cw_sb", [128, NDB, 4], F32)
    cb_sb = sbt("cb_sb", [128, NDB], F32)
    dp_sb = sbt("dp_sb", [128, NDB], F32)
    dtbias_sb = sbt("dtbias_sb", [128, NDB], F32)
    wdt_sb = sbt("wdt_sb", [128, 2, DC], BF16)
    wx_sb = sbt("wx_sb", [128, NDB, 288], BF16)
    selmat = sbt("selmat", [16, NST, 128], BF16)
    ident = sbt("ident", [128, 128], BF16)
    identf = sbt("identf", [128, 128], F32)
    ones1f = sbt("ones1f", [1, 128], F32)
    ones16x = sbt("ones16x", [16, 128], F32)
    onescol = sbt("onescol", [128, 1], BF16)
    onescol16 = sbt("onescol16", [16, 1], BF16)
    ones16f = sbt("ones16f", [1, 16], F32)
    p8mats = sbt("p8mats", [64, 8, 128], BF16)
    iotaf = sbt("iotaf", [128, 128], F32)
    pidxf = sbt("pidxf", [128, 1], F32)
    smalli = sbt("smalli", [128, 128], I32)
    blw_sb = sbt("blw_sb", [16, 2], F32)
    mask16 = sbt("mask16", [16, 2], F32)
    dpdiag = sbt("dpdiag", [128, NDB, 128], BF16)
    cst_sb = sbt("cst_sb", [128, 8], F32)

    arena = sbtA("arena", [128, 13440], F32)
    _arb = arena[:].bitcast(BF16)
    xt_sb = _arb[:, 0:16384].rearrange("p (a t) -> p a t", a=32)
    brep_sb = _arb[:, 0:8192].rearrange("p (n t) -> p n t", n=NST)
    crep_sb = _arb[:, 8192:16384].rearrange("p (n t) -> p n t", n=NST)
    dax_sb = arena[:, 8192:8192 + 2 * NQ * SEG].rearrange(
        "p (b q s) -> p b q s", b=2, q=NQ)
    vbs_sb = _arb[:, 20608:20608 + 2 * NQ * SEG].rearrange(
        "p (b q s) -> p b q s", b=2, q=NQ)
    ss_sb = _arb[:, 22720:22720 + 2 * NQ * SEG].rearrange(
        "p (b q s) -> p b q s", b=2, q=NQ)
    ud_sb = sbtA("ud_sb", [128, NDB, TC + 3], F32)
    halo_sb = sbtA("halo_sb", [128, NDB, 3], F32)
    acc_sb = sbtA("acc_sb", [128, TC], F32)
    sg_sb = sbtA("sg_sb", [128, NDB, TC], BF16)
    ub_sb = sbtA("ub_sb", [128, NDB, TC], BF16)
    v_db = sbtA("v_db", [128, 2, TC], BF16)
    yg_sb = sbtA("yg_sb", [128, NDB, TC], BF16)
    dt_sum = sbtA("dt_sum", [128, 2, TC], F32)
    dtq_sb = sbtA("dtq_sb", [128, 2, TC], BF16)
    dtn_sb = sbtA("dtn_sb", [128, 2, TC], BF16)
    b_sum = sbtA("b_sum", [16, TC], F32)
    c_sum = sbtA("c_sum", [16, TC], F32)
    bq_sb = sbtA("bq_sb", [16, 2, TC], BF16)
    bu_sb = sbtA("bu_sb", [16, TC], BF16)
    cu_sb = sbtA("cu_sb", [16, TC], BF16)
    relu_sb = sbtA("relu_sb", [128, TC], F32)
    abs_sb = sbtA("abs_sb", [128, 2, TC], F32)
    rsdt_sb = abs_sb[0:1, 0, :]
    rsb_sb = abs_sb[0:1, 1, :]
    rsc_sb = relu_sb[0:1, :]
    carry = sbtA("carry", [128, NDB, NST], F32)
    sc_sb = sbtA("sc_sb", [128, 2, NQ, TC], BF16)
    win_slab = sbtA("win_slab", [128, 2, 16, 128], BF16)
    wout_slab = sbtA("wout_slab", [128, 2, 8, 128], BF16)

    attn_st = ud_sb[:, 0:2, 0:TC]
    ps = [E(nc.psum_tensor(f"ps{i}", [128, TC], F32)) for i in range(8)]
    bank_guard = {}

    def pe_use_bank(bank):
        g = bank_guard.get(bank)
        if g is not None:
            wait("tensor", g[0], g[1])

    if lean_init < 3:
     run("sync", lambda e: e.dma_start(out=A_sb[:], in_=a_par.rearrange("(p db) n -> p db n", p=128)), inc=s_dma, amt=16)
    run("sync", lambda e: e.dma_start(out=cw_sb[:], in_=conv_w.rearrange("(p db) k -> p db k", p=128)), inc=s_dma, amt=16)
    run("sync", lambda e: e.dma_start(out=cb_sb[:], in_=conv_b.rearrange("(p db) -> p db", p=128)), inc=s_dma, amt=16)
    run("sync", lambda e: e.dma_start(out=dp_sb[:], in_=d_par.rearrange("(p db) -> p db", p=128)), inc=s_dma, amt=16)
    run("sync", lambda e: e.dma_start(out=dtbias_sb[:], in_=dt_bias.rearrange("(p db) -> p db", p=128)), inc=s_dma, amt=16)
    run("sync", lambda e: e.dma_start(out=blw_sb[:, 0:1], in_=b_ln_w[:, None]), inc=s_dma, amt=16)
    run("sync", lambda e: e.dma_start(out=blw_sb[:, 1:2], in_=c_ln_w[:, None]), inc=s_dma, amt=16)
    wdt_stage = ud_sb[:].rearrange("p a t -> p (a t)")[:, 0:2 * DC].rearrange(
        "p (rb d) -> p rb d", rb=2)
    run("sync", lambda e: e.dma_start(out=wdt_stage, in_=wdtT.rearrange("(p rb) d -> p rb d", p=128)), inc=s_dma, amt=16)
    dtw_stage = acc_sb[:, 0:2]
    run("sync", lambda e: e.dma_start(out=dtw_stage, in_=dt_ln_w.rearrange("(p rb) -> p rb", p=128)), inc=s_dma, amt=16)
    wx_stage1 = yg_sb[:].rearrange("p a t -> p (a t)").bitcast(F32)[:, 0:12 * 288].rearrange(
        "p (db o) -> p db o", db=12)
    wx_stage2 = ud_sb[:].rearrange("p a t -> p (a t)")[:, 4096:4096 + 4 * 288].rearrange(
        "p (db o) -> p db o", db=4)
    run("sync", lambda e: e.dma_start(
        out=wx_stage1, in_=wxT[0:12 * 128 if False else 0:1536, :].rearrange(
            "(p db) o -> p db o", p=128) if False else wxT.rearrange(
            "(p db) o -> p db o", p=128)[:, 0:12, :]), inc=s_dma, amt=16)
    run("sync", lambda e: e.dma_start(
        out=wx_stage2, in_=wxT.rearrange("(p db) o -> p db o", p=128)[:, 12:16, :]), inc=s_dma, amt=16)
    n_dma0 = s_dma.n

    n_wprep = 0

    run("gpsimd", lambda e: e.iota(smalli[:], pattern=[[1, 128]], base=0,
                                   channel_multiplier=0), inc=s_gp)
    g1 = s_gp.n
    wait("vector", s_gp, g1)
    run("vector", lambda e: e.tensor_copy(out=iotaf[:], in_=smalli[:]), inc=s_dv)
    v1 = s_dv.n
    wait("gpsimd", s_dv, v1)
    run("gpsimd", lambda e: e.iota(smalli[:, 0:1], pattern=[[0, 1]], base=0,
                                   channel_multiplier=1), inc=s_gp)
    g2 = s_gp.n
    wait("vector", s_gp, g2)
    wait("vector", s_dv, v1)
    run("vector", lambda e: e.tensor_copy(out=pidxf[:], in_=smalli[:, 0:1]), inc=s_dv)
    v2 = s_dv.n
    wait("vector", s_dv, v2)
    run("vector", lambda e: e.tensor_single_scalar(out=ident[:], in_=iotaf[:], scalar=pidxf[:, 0:1], op=ALU.is_equal), inc=s_dv)
    run("vector", lambda e: e.tensor_single_scalar(out=identf[:], in_=iotaf[:], scalar=pidxf[:, 0:1], op=ALU.is_equal), inc=s_dv)
    run("vector", lambda e: e.memset(ones1f[:], 1.0), inc=s_dv)
    run("vector", lambda e: e.memset(ones16x[:], 1.0), inc=s_dv)
    run("vector", lambda e: e.memset(onescol[:], 1.0), inc=s_dv)
    run("vector", lambda e: e.memset(onescol16[:], 1.0), inc=s_dv)
    run("vector", lambda e: e.memset(ones16f[:], 1.0), inc=s_dv)
    for col, val in ((0, 1e-5), (1, 1e-6), (2, 1.0 / RDT), (3, 1.0 / NST),
                     (4, 1.0), (5, -1.0)):
        run("vector", lambda e, col=col, val=val: e.memset(cst_sb[:, col:col + 1], val), inc=s_dv)
    run("vector", lambda e: e.tensor_scalar(out=abs_sb[0:64, 0, 0:128], in0=iotaf[0:64, :], scalar1=7.5, scalar2=1.0 / 16.0, op0=ALU.subtract, op1=ALU.mult), inc=s_dv)
    v3 = s_dv.n
    wait("vector", s_dv, v3)
    run("vector", lambda e: e.tensor_copy(out=smalli[0:64, 0:128], in_=abs_sb[0:64, 0, 0:128]), inc=s_dv)
    v4 = s_dv.n
    wait("vector", s_dv, v4)
    run("vector", lambda e: e.tensor_copy(out=abs_sb[0:64, 0, 0:128], in_=smalli[0:64, 0:128]), inc=s_dv)
    v5 = s_dv.n
    wait("vector", s_dv, v5)
    for kb in range(8):
        run("vector", lambda e, kb=kb: e.tensor_scalar(
            out=p8mats[:, kb, :], in0=abs_sb[0:64, 0, 0:128],
            scalar1=float(8 * kb), scalar2=pidxf[0:64, 0:1],
            op0=ALU.add, op1=ALU.is_equal), inc=s_dv)
    wait("vector", s_dma, n_dma0)
    for n in range(NST):
        run("vector", lambda e, n=n: e.tensor_single_scalar(
            out=mask16[:, 0:1], in_=pidxf[0:16, 0:1], scalar=float(n), op=ALU.is_equal), inc=s_dv)
        vm = s_dv.n
        wait("vector", s_dv, vm)
        run("vector", lambda e, n=n: e.tensor_scalar(
            out=selmat[:, n, :], in0=ones16x[:], scalar1=mask16[:, 0:1],
            scalar2=None, op0=ALU.mult), inc=s_dv)
    for rb in range(2):
        run("vector", lambda e, rb=rb: e.tensor_scalar(
            out=wdt_sb[:, rb, :], in0=wdt_stage[:, rb, :],
            scalar1=dtw_stage[:, rb:rb + 1], scalar2=None, op0=ALU.mult), inc=s_dv)
    run("vector", lambda e: e.tensor_copy(out=wx_sb[:, 0:12, :], in_=wx_stage1), inc=s_dv)
    run("vector", lambda e: e.tensor_copy(out=wx_sb[:, 12:16, :], in_=wx_stage2), inc=s_dv)
    for db in range(NDB):
        run("vector", lambda e, db=db: e.tensor_scalar(
            out=dpdiag[:, db, :], in0=ident[:], scalar1=dp_sb[:, db:db + 1],
            scalar2=None, op0=ALU.mult), inc=s_dv)
    run("vector", lambda e: e.memset(carry[:], 0.0), inc=s_dv)
    run("vector", lambda e: e.memset(halo_sb[:], 0.0), inc=s_dv)
    v_wfold = s_dv.n
    wait("vector", s_dv, v_wfold)
    run("vector", lambda e: e.memset(dax_sb[:, :, :, 0:1], 0.0), inc=s_dv)
    run("vector", lambda e: e.memset(dax_sb[:, :, :, 1:2], 1.0), inc=s_dv)
    run("vector", lambda e: e.memset(vbs_sb[:, :, :, 1:2], 0.0), inc=s_dv)
    v_init = s_dv.n

    lnx = arena[:, 0:8192].rearrange("p (a t) -> p a t", a=2)
    lnb = sg_sb[:].rearrange("p a t -> p (a t)").rearrange("p (a t) -> p a t", a=2)
    sqx = yg_sb[:].rearrange("p a t -> p (a t)").bitcast(F32)
    xTq = ub_sb[:].rearrange("p a t -> p (a t)").rearrange("p (a t) -> p a t", a=32)
    with ExitStack() as es1:
        stat = es1.enter_context(nc.sbuf_tensor("stat", [128, 8], F32))
        wait("sync", s_dv, v_init)
        wait("scalar", s_dv, v_init)
        n_tiles = TOWNE // 128
        n_groups = (n_tiles + 1) // 2
        for g in range(n_groups):
            gsz = min(2, n_tiles - 2 * g)
            if g > 0:
                wait("sync", s_pe, s_pe.n)
                wait("sync", s_dma, s_dma.n)
            for i in range(gsz):
                ti = 2 * g + i
                run("sync", lambda e, ti=ti, i=i: e.dma_start(
                    out=lnx[:, i, :], in_=x_own[128 * ti:128 * (ti + 1), :]),
                    inc=s_dma, amt=16)
            wait("vector", s_dma, s_dma.n)
            wait("scalar", s_dma, s_dma.n)
            wait("vector", s_dv, v_init)
            for i in range(gsz):
                run("vector", lambda e, i=i: e.tensor_reduce(
                    out=stat[:, 0:1], in_=lnx[:, i, :], axis=AX.X, op=ALU.add), inc=s_dv)
                run("scalar", lambda e, i=i: e.activation(
                    out=sqx[:], in_=lnx[:, i, :], func=AF.Square), inc=s_ac)
                va = s_ac.n
                wait("vector", s_ac, va)
                run("vector", lambda e: e.tensor_reduce(
                    out=stat[:, 1:2], in_=sqx[:], axis=AX.X, op=ALU.add), inc=s_dv)
                vr = s_dv.n
                wait("vector", s_dv, vr)
                run("vector", lambda e: e.tensor_scalar(
                    out=stat[:, 2:3], in0=stat[:, 0:1], scalar1=1.0 / H, scalar2=None, op0=ALU.mult), inc=s_dv)
                run("vector", lambda e: e.tensor_scalar(
                    out=stat[:, 4:5], in0=stat[:, 1:2], scalar1=1.0 / H, scalar2=None, op0=ALU.mult), inc=s_dv)
                vr2 = s_dv.n
                wait("vector", s_dv, vr2)
                run("vector", lambda e: e.tensor_tensor(
                    out=stat[:, 3:4], in0=stat[:, 2:3], in1=stat[:, 2:3], op=ALU.mult), inc=s_dv)
                vr3 = s_dv.n
                wait("vector", s_dv, vr3)
                run("vector", lambda e: e.tensor_tensor(
                    out=stat[:, 5:6], in0=stat[:, 4:5], in1=stat[:, 3:4], op=ALU.subtract), inc=s_dv)
                vr4 = s_dv.n
                wait("scalar", s_dv, vr4)
                run("scalar", lambda e: e.activation(
                    out=stat[:, 6:7], in_=stat[:, 5:6], func=AF.Sqrt, bias=cst_sb[:, 0:1]), inc=s_ac)
                va2 = s_ac.n
                wait("vector", s_ac, va2)
                run("vector", lambda e: e.reciprocal(out=stat[:, 7:8], in_=stat[:, 6:7]), inc=s_dv)
                vr5 = s_dv.n
                wait("vector", s_dv, vr5)
                run("vector", lambda e, i=i: e.tensor_scalar(
                    out=lnb[:, i, :], in0=lnx[:, i, :], scalar1=stat[:, 2:3],
                    scalar2=stat[:, 7:8], op0=ALU.subtract, op1=ALU.mult), inc=s_dv)
            v_ln = s_dv.n
            wait("tensor", s_dv, v_ln)
            for j in range(32):
                bank = j % 2
                pe_use_bank(bank)
                for i in range(gsz):
                    run("tensor", lambda e, i=i, j=j, bank=bank: e.transpose(
                        ps[bank][:].bitcast(BF16)[:, 128 * i:128 * (i + 1)],
                        lnb[:, i, 128 * j:128 * (j + 1)], ident[:]),
                        inc=s_pe, amt=(1 if i == gsz - 1 else 0))
                pj = s_pe.n
                wait("scalar", s_pe, pj)
                ev = run("scalar", lambda e, j=j, bank=bank, gsz=gsz: e.activation(
                    out=xTq[:, j, 0:128 * gsz],
                    in_=ps[bank][:].bitcast(BF16)[:, 0:128 * gsz],
                    func=AF.Copy), inc=s_ac)
                bank_guard[bank] = (s_ac, ev)
            wait("sync", s_ac, s_ac.n)
            run("sync", lambda e, g=g, gsz=gsz: e.dma_start(
                out=ag_in[:, 256 * g:256 * g + 128 * gsz].rearrange("(hb p) t -> p hb t", p=128),
                in_=xTq[:, :, 0:128 * gsz]), inc=s_dma, amt=16)
        wait("gpsimd", s_dma, s_dma.n)
        if not skip_ag:
            run("gpsimd", lambda e: e.collective_compute(
                "AllGather", ALU.bypass, replica_groups=GROUPS,
                ins=[ag_in.opt()], outs=[ag_out.opt()]), inc=s_cc)
        cc_ag = s_cc.n
        for dst, src in ((w_inT_bf, w_inT), (w_outT_bf, w_outT),
                         (ffn_aT_bf, ffn_aT), (ffn_bT_bf, ffn_bT),
                         (ffn_wrT_bf, ffn_wrT)):
            run("gpsimd", lambda e, d=dst, s=src: e.dma_start(out=d, in_=s),
                inc=s_wprep, amt=16)
        if "agin" in taps:
            wait("sync", s_dma, s_dma.n)
            run("sync", lambda e: e.dma_start(
                out=tap_aps["agin"], in_=ag_in[:, 0:min(TC, TOWNE)]), inc=s_st, amt=16)
        if "agout" in taps:
            wait("sync", s_cc, cc_ag)
            run("sync", lambda e: e.dma_start(
                out=tap_aps["agout"], in_=ag_out[0:H, 0:min(TC, TOWNE)]), inc=s_st, amt=16)

    if only_stage1:
        wait("sync", s_cc, cc_ag)
        wait("sync", s_st, s_st.n)
        with nc.Block() as block:
            @block.sync
            def _(eng):
                for f in plan["sync"]:
                    f(eng)

            @block.vector
            def _(eng):
                for f in plan["vector"]:
                    f(eng)

            @block.scalar
            def _(eng):
                for f in plan["scalar"]:
                    f(eng)

            @block.tensor
            def _(eng):
                for f in plan["tensor"]:
                    f(eng)

            @block.gpsimd
            def _(eng):
                for f in plan["gpsimd"]:
                    f(eng)
        esA.close()
        es.close()
        return nc

    def load_win_half(cb, half, buf):
        def f(e):
            r0 = 128 * (2 * cb + half)
            return e.dma_start(
                out=win_slab[:, buf, :, :],
                in_=w_inT_bf[r0:r0 + 128, :].rearrange(
                    "p (hb c) -> p hb c", hb=16))
        return f

    def load_wout_half(hb, half, buf):
        def f(e):
            r0 = 128 * (2 * hb + half)
            return e.dma_start(
                out=wout_slab[:, buf, :, :],
                in_=w_outT_bf[r0:r0 + 128, :].rearrange(
                    "p (db c) -> p db c", db=8))
        return f

    wait("sync", s_wprep, 32)
    win_vals = []
    wout_vals = []
    win_vals.append(run("sync", load_win_half(0, 0, 0), inc=s_win, amt=16))
    win_vals.append(run("sync", load_win_half(0, 1, 1), inc=s_win, amt=16))
    wout_vals.append(run("sync", load_wout_half(0, 0, 0), inc=s_wout, amt=16))
    wout_vals.append(run("sync", load_wout_half(0, 1, 1), inc=s_wout, amt=16))

    tap_done = set()
    stage_free = {0: (s_gp, 0), 1: (s_gp, 0)}
    sc_free = {0: (s_pe, 0), 1: (s_pe, 0)}
    pe_win_done_prev = 0
    v_yg_prev = 0
    gp_cmul_prev = 0
    vdb_free = {0: 0, 1: 0}

    for c in range(n_chunks):
        wait("sync", s_cc, cc_ag)
        if c > 0:
            wait("sync", s_pe, pe_win_done_prev)
            wait("sync", s_dv, v_yg_prev)
            wait("sync", s_gp, gp_cmul_prev)
        for q in range(4):
            run("sync", lambda e, q=q, c=c: e.dma_start(
                out=xt_sb[:, :, 128 * q:128 * (q + 1)],
                in_=ag_out[H * q:H * (q + 1), 128 * c:128 * (c + 1)].rearrange(
                    "(hb p) t -> p hb t", p=128)), inc=s_xt, amt=16)
        dma_xt = s_xt.n

        wait("vector", s_dv, v_init)
        wait("scalar", s_dv, v_yg_prev)
        wait("scalar", s_gp, gp_cmul_prev)

        wait("tensor", s_xt, dma_xt)
        for cb in range(32):
            bank = cb % 2
            pe_use_bank(bank)
            for half in range(2):
                k = 64 * c + 2 * cb + half
                buf = k % 2
                wait("tensor", s_win, win_vals[k])
                for hb in range(16):
                    run("tensor", lambda e, hb=hb, half=half, buf=buf, bank=bank: e.matmul(
                        ps[bank][:], win_slab[:, buf, hb, :],
                        xt_sb[:, 16 * half + hb, :],
                        start=(half == 0 and hb == 0), stop=(half == 1 and hb == 15)),
                        inc=(s_pe if hb == 15 else None))
                half_done = s_pe.n
                nxt = k + 2
                if nxt < 64 * n_chunks:
                    ncb, nhalf = (nxt % 64) // 2, nxt % 2
                    wait("sync", s_pe, half_done)
                    win_vals.append(run("sync", load_win_half(ncb, nhalf, buf),
                                        inc=s_win, amt=16))
            pe_done = s_pe.n
            wait("scalar", s_pe, pe_done)
            if cb < 16:
                ev = run("scalar", lambda e, cb=cb, bank=bank: e.activation(
                    out=ud_sb[:, cb, 3:3 + TC], in_=ps[bank][:], func=AF.Copy), inc=s_ac)
            else:
                ev = run("scalar", lambda e, cb=cb, bank=bank: e.activation(
                    out=sg_sb[:, cb - 16, :], in_=ps[bank][:], func=AF.Silu), inc=s_ac)
            bank_guard[bank] = (s_ac, ev)
        pe_win_done_prev = s_pe.n
        wait("vector", s_ac, s_ac.n)
        run("vector", lambda e: e.tensor_copy(out=ud_sb[:, :, 0:3], in_=halo_sb[:]), inc=s_dv)
        v_halo = s_dv.n

        if c == 0 and "ulin" in taps and "ulin" not in tap_done:
            tap_done.add("ulin")
            wait("sync", s_dv, v_halo)
            run("sync", lambda e: e.dma_start(out=tap_aps["ulin"], in_=ud_sb[:]), inc=s_st, amt=16)
            em("vector", lambda e, s=s_st.h, vv=s_st.n: e.wait_ge(s, vv))

        prev_silu = 0
        for db in range(NDB):
            wait("vector", s_dv, v_halo)
            if prev_silu:
                wait("vector", s_ac, prev_silu)
            run("vector", lambda e, db=db: e.tensor_scalar(
                out=acc_sb[:], in0=ud_sb[:, db, 0:TC],
                scalar1=cw_sb[:, db, 0:1], scalar2=None, op0=ALU.mult), inc=s_dv)
            for kk in range(1, 4):
                vp = s_dv.n
                wait("vector", s_dv, vp)
                run("vector", lambda e, db=db, kk=kk: e.scalar_tensor_tensor(
                    out=acc_sb[:], in0=ud_sb[:, db, kk:kk + TC],
                    scalar=cw_sb[:, db, kk:kk + 1], in1=acc_sb[:],
                    op0=ALU.mult, op1=ALU.add), inc=s_dv)
            v_conv = s_dv.n
            wait("scalar", s_dv, v_conv)
            run("scalar", lambda e, db=db: e.activation(
                out=ub_sb[:, db, :], in_=acc_sb[:], func=AF.Silu,
                bias=cb_sb[:, db:db + 1]), inc=s_ac)
            ac_u = s_ac.n
            prev_silu = ac_u
        ac_ub = s_ac.n
        wait("vector", s_ac, ac_ub)
        run("vector", lambda e: e.tensor_copy(out=halo_sb[:], in_=ud_sb[:, :, TC:TC + 3]), inc=s_dv)
        v_hsave = s_dv.n

        if c == 0:
            for t, buf, gate in (("ub", ub_sb, (s_ac, ac_ub)),
                                 ("sg", sg_sb, (s_ac, ac_ub)),
                                 ("xt", xt_sb, (s_xt, dma_xt))):
                if t in taps and t not in tap_done:
                    tap_done.add(t)
                    wait("sync", gate[0], gate[1])
                    run("sync", lambda e, t=t, buf=buf: e.dma_start(
                        out=tap_aps[t], in_=buf[:]), inc=s_st, amt=16)

        ssm_banks = (4, 5, 6, 7)
        for ob in range(4):
            bank = ssm_banks[ob]
            pe_use_bank(bank)
            wait("tensor", s_ac, ac_ub)
            if ob < 2:
                sl = lambda db, ob=ob: wx_sb[:, db, 128 * ob:128 * (ob + 1)]
                op_full = True
            elif ob == 2:
                sl = lambda db: wx_sb[:, db, 256:272]
                op_full = False
            else:
                sl = lambda db: wx_sb[:, db, 272:288]
                op_full = False
            for db in range(NDB):
                run("tensor", lambda e, db=db, sl=sl, bank=bank, op_full=op_full: e.matmul(
                    ps[bank][:] if op_full else ps[bank][0:16, :],
                    sl(db), ub_sb[:, db, :],
                    start=(db == 0), stop=(db == NDB - 1)),
                    inc=(s_pe if db == NDB - 1 else None))
            pe_s = s_pe.n
            wait("scalar", s_pe, pe_s)
            if c > 0:
                wait("scalar", s_shtl, ssm_store_done)
            if ob < 2:
                ev = run("scalar", lambda e, ob=ob, bank=bank: e.activation(
                    out=dt_sum[:, ob, :], in_=ps[bank][:], func=AF.Copy), inc=s_ac)
            elif ob == 2:
                ev = run("scalar", lambda e, bank=bank: e.activation(
                    out=b_sum[:], in_=ps[bank][0:16, :], func=AF.Copy), inc=s_ac)
            else:
                ev = run("scalar", lambda e, bank=bank: e.activation(
                    out=c_sum[:], in_=ps[bank][0:16, :], func=AF.Copy), inc=s_ac)
            bank_guard[bank] = (s_ac, ev)
        ac_ssm = s_ac.n
        wait("sync", s_ac, ac_ssm)
        run("sync", lambda e, c=c: e.dma_start(
            out=ssm_in[c][0:256, :].rearrange("(rb p) t -> p rb t", p=128),
            in_=dt_sum[:]), inc=s_shtl, amt=16)
        run("sync", lambda e, c=c: e.dma_start(out=ssm_in[c][256:272, :], in_=b_sum[:]), inc=s_shtl, amt=16)
        run("sync", lambda e, c=c: e.dma_start(out=ssm_in[c][272:288, :], in_=c_sum[:]), inc=s_shtl, amt=16)
        ssm_store_done = s_shtl.n
        wait("gpsimd", s_shtl, ssm_store_done)
        run("gpsimd", lambda e, c=c: e.collective_compute(
            "AllReduce", ALU.add, replica_groups=GROUPS,
            ins=[ssm_in[c].opt()], outs=[ssm_out[c].opt()]), inc=s_cc)
        wait("sync", s_cc, s_cc.n)
        run("sync", lambda e, c=c: e.dma_start(
            out=dt_sum[:], in_=ssm_out[c][0:256, :].rearrange("(rb p) t -> p rb t", p=128)), inc=s_shtl, amt=16)
        run("sync", lambda e, c=c: e.dma_start(out=b_sum[:], in_=ssm_out[c][256:272, :]), inc=s_shtl, amt=16)
        run("sync", lambda e, c=c: e.dma_start(out=c_sum[:], in_=ssm_out[c][272:288, :]), inc=s_shtl, amt=16)
        dma_ssm = s_shtl.n

        wait("scalar", s_shtl, dma_ssm)
        for rb in range(2):
            run("scalar", lambda e, rb=rb: e.activation(
                out=dtq_sb[:, rb, :], in_=dt_sum[:, rb, :], func=AF.Square), inc=s_ac)
        run("scalar", lambda e: e.activation(out=bq_sb[:, 0, :], in_=b_sum[:], func=AF.Square), inc=s_ac)
        run("scalar", lambda e: e.activation(out=bq_sb[:, 1, :], in_=c_sum[:], func=AF.Square), inc=s_ac)
        ac_sq = s_ac.n
        for bk in (0, 1, 2):
            pe_use_bank(bk)
        wait("tensor", s_ac, ac_sq)
        for rb in range(2):
            run("tensor", lambda e, rb=rb: e.matmul(
                ps[0][0:1, :], onescol[:], dtq_sb[:, rb, :],
                start=(rb == 0), stop=(rb == 1)), inc=(s_pe if rb == 1 else None))
        run("tensor", lambda e: e.matmul(ps[1][0:1, :], onescol16[:], bq_sb[0:16, 0, :], start=True, stop=True), inc=s_pe)
        run("tensor", lambda e: e.matmul(ps[2][0:1, :], onescol16[:], bq_sb[0:16, 1, :], start=True, stop=True), inc=s_pe)
        pe_sq = s_pe.n
        wait("scalar", s_pe, pe_sq)
        run("scalar", lambda e: e.activation(out=rsdt_sb, in_=ps[0][0:1, :], func=AF.Sqrt, scale=cst_sb[0:1, 2:3], bias=cst_sb[0:1, 1:2]), inc=s_ac)
        run("scalar", lambda e: e.activation(out=rsb_sb, in_=ps[1][0:1, :], func=AF.Sqrt, scale=cst_sb[0:1, 3:4], bias=cst_sb[0:1, 1:2]), inc=s_ac)
        run("scalar", lambda e: e.activation(out=rsc_sb, in_=ps[2][0:1, :], func=AF.Sqrt, scale=cst_sb[0:1, 3:4], bias=cst_sb[0:1, 1:2]), inc=s_ac)
        ac_rs = s_ac.n
        for bk in (0, 1, 2):
            bank_guard[bk] = (s_ac, ac_rs)
        wait("vector", s_ac, ac_rs)
        run("vector", lambda e: e.reciprocal(out=rsdt_sb, in_=rsdt_sb), inc=s_dv)
        run("vector", lambda e: e.reciprocal(out=rsb_sb, in_=rsb_sb), inc=s_dv)
        run("vector", lambda e: e.reciprocal(out=rsc_sb, in_=rsc_sb), inc=s_dv)
        v_rs = s_dv.n
        for bk in (0, 1, 2):
            pe_use_bank(bk)
        wait("tensor", s_dv, v_rs)
        run("tensor", lambda e: e.matmul(ps[0][:], ones1f[:], rsdt_sb, start=True, stop=True), inc=s_pe)
        run("tensor", lambda e: e.matmul(ps[1][0:16, :], ones16f[0:1, 0:16], rsb_sb, start=True, stop=True), inc=s_pe)
        run("tensor", lambda e: e.matmul(ps[2][0:16, :], ones16f[0:1, 0:16], rsc_sb, start=True, stop=True), inc=s_pe)
        pe_rep = s_pe.n
        wait("vector", s_pe, pe_rep)
        for rb in range(2):
            run("vector", lambda e, rb=rb: e.tensor_tensor(
                out=dtn_sb[:, rb, :], in0=dt_sum[:, rb, :], in1=ps[0][:], op=ALU.mult), inc=s_dv)
        run("vector", lambda e: e.scalar_tensor_tensor(
            out=bu_sb[:], in0=b_sum[:], scalar=blw_sb[:, 0:1], in1=ps[1][0:16, :],
            op0=ALU.mult, op1=ALU.mult), inc=s_dv)
        run("vector", lambda e: e.scalar_tensor_tensor(
            out=cu_sb[:], in0=c_sum[:], scalar=blw_sb[:, 1:2], in1=ps[2][0:16, :],
            op0=ALU.mult, op1=ALU.mult), inc=s_dv)
        v_norm = s_dv.n
        for bk in (0, 1, 2):
            bank_guard[bk] = (s_dv, v_norm)

        if c == 0:
            for t, buf in (("dtn", dt_sum), ("bu", b_sum), ("cu", c_sum)):
                if t in taps and t not in tap_done:
                    tap_done.add(t)
                    wait("sync", s_dv, v_norm)
                    run("sync", lambda e, t=t, buf=buf: e.dma_start(
                        out=tap_aps[t], in_=buf[:]), inc=s_st, amt=16)

        wait("scalar", s_pe, pe_win_done_prev)
        for n in range(NST):
            pe_use_bank(3)
            wait("tensor", s_dv, v_norm)
            run("tensor", lambda e, n=n: e.matmul(
                ps[3][:], selmat[:, n, :], bu_sb[:], start=True, stop=True), inc=s_pe)
            pb = s_pe.n
            wait("scalar", s_pe, pb)
            ev = run("scalar", lambda e, n=n: e.activation(
                out=brep_sb[:, n, :], in_=ps[3][:], func=AF.Copy), inc=s_ac)
            bank_guard[3] = (s_ac, ev)
            pe_use_bank(3)
            run("tensor", lambda e, n=n: e.matmul(
                ps[3][:], selmat[:, n, :], cu_sb[:], start=True, stop=True), inc=s_pe)
            pcc = s_pe.n
            wait("scalar", s_pe, pcc)
            ev = run("scalar", lambda e, n=n: e.activation(
                out=crep_sb[:, n, :], in_=ps[3][:], func=AF.Copy), inc=s_ac)
            bank_guard[3] = (s_ac, ev)
        ac_rep = s_ac.n

        wait("vector", s_dv, v_hsave)
        prev_dadd = 0
        for db in range(NDB):
            bank = 4 + (db % 2)
            pe_use_bank(bank)
            wait("tensor", s_dv, v_norm)
            for rb in range(2):
                run("tensor", lambda e, db=db, rb=rb, bank=bank: e.matmul(
                    ps[bank][:], wdt_sb[:, rb, 128 * db:128 * (db + 1)],
                    dtn_sb[:, rb, :], start=(rb == 0), stop=(rb == 1)),
                    inc=(s_pe if rb == 1 else None))
            pd = s_pe.n
            wait("scalar", s_pe, pd)
            if prev_dadd:
                wait("scalar", s_dv, prev_dadd)
            run("scalar", lambda e, db=db, bank=bank: e.activation(
                out=relu_sb[:], in_=ps[bank][:], func=AF.Relu,
                bias=dtbias_sb[:, db:db + 1]), inc=s_ac)
            ev = run("scalar", lambda e, db=db, bank=bank: e.activation(
                out=abs_sb[:, 0, :], in_=ps[bank][:], func=AF.Abs,
                bias=dtbias_sb[:, db:db + 1]), inc=s_ac)
            bank_guard[bank] = (s_ac, ev)
            a1 = s_ac.n
            wait("scalar", s_ac, a1)
            run("scalar", lambda e: e.activation(
                out=abs_sb[:, 1, :], in_=abs_sb[:, 0, :], func=AF.Exp, scale=cst_sb[:, 5:6]), inc=s_ac)
            a2 = s_ac.n
            wait("scalar", s_ac, a2)
            run("scalar", lambda e: e.activation(
                out=abs_sb[:, 0, :], in_=abs_sb[:, 1, :], func=AF.Ln, bias=cst_sb[:, 4:5]), inc=s_ac)
            a3 = s_ac.n
            wait("vector", s_ac, a3)
            run("vector", lambda e, db=db: e.tensor_tensor(
                out=ud_sb[:, db, 0:TC], in0=relu_sb[:], in1=abs_sb[:, 0, :],
                op=ALU.add), inc=s_dv)
            prev_dadd = s_dv.n
        v_vready = s_dv.n

        if c == 0 and "delta" in taps and "delta" not in tap_done:
            tap_done.add("delta")
            wait("sync", s_dv, v_vready)
            run("sync", lambda e: e.dma_start(
                out=tap_aps["delta"], in_=ud_sb[:, :, 0:TC]), inc=s_st, amt=16)

        for db in range(NDB):
            ybank = 6 + (db % 2)
            pe_use_bank(ybank)
            wait("tensor", s_ac, ac_ub)
            run("tensor", lambda e, db=db, ybank=ybank: e.matmul(
                ps[ybank][:], dpdiag[:, db, :], ub_sb[:, db, :],
                start=True, stop=False), inc=s_pe)
            pb = db % 2
            wait("vector", s_dv, vdb_free[pb])
            wait("vector", s_dv, v_vready)
            run("vector", lambda e, db=db, pb=pb: e.tensor_tensor(
                out=v_db[:, pb, :], in0=ud_sb[:, db, 0:TC], in1=ub_sb[:, db, :],
                op=ALU.mult), inc=s_dv)
            v_vdb = s_dv.n
            for q in range(NGRP):
                it = (c * NDB + db) * NGRP + q
                qb = it % 2
                n0 = NQ * q
                sem_f, val_f = stage_free[qb]
                wait("scalar", sem_f, val_f)
                wait("scalar", s_dv, v_vready)
                wait("scalar", s_ac, ac_rep)
                for j in range(NQ):
                    run("scalar", lambda e, db=db, j=j, qb=qb, n0=n0: e.activation(
                        out=dax_sb[:, qb, j, 2:2 + TC], in_=ud_sb[:, db, 0:TC],
                        func=AF.Exp, scale=A_sb[:, db, n0 + j:n0 + j + 1]), inc=s_ac)
                ac_dax = s_ac.n
                wait("vector", sem_f, val_f)
                wait("vector", s_ac, ac_rep)
                wait("vector", s_dv, v_vdb)
                v_bc, b_bc = broadcast_tensor_aps(
                    v_db[:, pb, :][:, None, :], brep_sb[:, n0:n0 + NQ, :])
                run("vector", lambda e, qb=qb, v_bc=v_bc, b_bc=b_bc: e.tensor_tensor(
                    out=vbs_sb[:, qb, :, 2:2 + TC], in0=v_bc, in1=b_bc,
                    op=ALU.mult), inc=s_dv)
                run("vector", lambda e, db=db, qb=qb, n0=n0: e.tensor_copy(
                    out=vbs_sb[:, qb, :, 0:1],
                    in_=carry[:, db, n0:n0 + NQ][:, :, None]), inc=s_dv)
                v_vb = s_dv.n
                wait("vector", s_ac, ac_dax)
                wait("vector", s_dv, v_vb)
                run("vector", lambda e, qb=qb: e.tensor_tensor_scan(
                    out=ss_sb[:, qb, :, :].rearrange("p a s -> p (a s)"),
                    data0=dax_sb[:, qb, :, :].rearrange("p a s -> p (a s)"),
                    data1=vbs_sb[:, qb, :, :].rearrange("p a s -> p (a s)"),
                    initial=0.0, op0=ALU.mult, op1=ALU.add), inc=s_dv)
                v_scan = s_dv.n
                wait("vector", s_dv, v_scan)
                run("vector", lambda e, db=db, qb=qb, n0=n0: e.tensor_copy(
                    out=carry[:, db, n0:n0 + NQ],
                    in_=ss_sb[:, qb, :, SEG - 1:SEG].rearrange("p a o -> p (a o)")), inc=s_dv)
                sem_s, val_s = sc_free[qb]
                wait("gpsimd", sem_s, val_s)
                wait("gpsimd", s_dv, v_scan)
                wait("gpsimd", s_ac, ac_rep)
                run("gpsimd", lambda e, qb=qb, n0=n0: e.tensor_tensor(
                    out=sc_sb[:, qb, :, :], in0=ss_sb[:, qb, :, 2:2 + TC],
                    in1=crep_sb[:, n0:n0 + NQ, :], op=ALU.mult), inc=s_gp)
                gp_sc = s_gp.n
                stage_free[qb] = (s_gp, gp_sc)
                wait("tensor", s_gp, gp_sc)
                last = (q == NGRP - 1)
                for j in range(NQ):
                    run("tensor", lambda e, qb=qb, j=j, ybank=ybank,
                        st=(last and j == NQ - 1): e.matmul(
                        ps[ybank][:], ident[:], sc_sb[:, qb, j, :],
                        start=False, stop=st),
                        inc=(s_pe if (last and j == NQ - 1) else None))
                if not last:
                    run("tensor", lambda e: e.nop(), inc=s_pe)
                sc_free[qb] = (s_pe, s_pe.n)
            pe_y = s_pe.n
            vdb_free[pb] = s_dv.n
            wait("vector", s_pe, pe_y)
            ev = run("vector", lambda e, db=db, ybank=ybank: e.tensor_tensor(
                out=yg_sb[:, db, :], in0=ps[ybank][:], in1=sg_sb[:, db, :],
                op=ALU.mult), inc=s_dv)
            bank_guard[ybank] = (s_dv, ev)
        v_yg = s_dv.n
        v_yg_prev = v_yg
        gp_cmul_prev = s_gp.n

        if c == 0:
            for t, buf, gate in (("yg", yg_sb, (s_dv, v_yg)),
                                 ("carry", carry, (s_dv, v_yg))):
                if t in taps and t not in tap_done:
                    tap_done.add(t)
                    wait("sync", gate[0], gate[1])
                    run("sync", lambda e, t=t, buf=buf: e.dma_start(
                        out=tap_aps[t], in_=buf[:]), inc=s_st, amt=16)

        if c == 0 and "slab0" in taps and "slab0" not in tap_done:
            tap_done.add("slab0")
            wait("sync", s_wout, wout_vals[0])
            run("sync", lambda e: e.dma_start(out=tap_aps["slab0"], in_=wout_slab[:, 0, :, :]), inc=s_st, amt=16)
        if c == 0 and "yg2" in taps and "yg2" not in tap_done:
            tap_done.add("yg2")
            wait("sync", s_dv, v_yg)
            run("sync", lambda e: e.dma_start(out=tap_aps["yg2"], in_=yg_sb[:]), inc=s_st, amt=16)
        att_store = []
        for hb in range(32):
            bank = hb % 2
            pe_use_bank(bank)
            wait("tensor", s_dv, v_yg)
            for half in range(2):
                k = 64 * c + 2 * hb + half
                buf = k % 2
                wait("tensor", s_wout, wout_vals[k])
                for dbh in range(8):
                    db = 8 * half + dbh
                    run("tensor", lambda e, dbh=dbh, db=db, buf=buf, bank=bank, half=half: e.matmul(
                        ps[bank][:], wout_slab[:, buf, dbh, :], yg_sb[:, db, :],
                        start=(half == 0 and dbh == 0), stop=(half == 1 and dbh == 7)),
                        inc=(s_pe if dbh == 7 else None))
                half_done = s_pe.n
                nxt = k + 2
                if nxt < 64 * n_chunks:
                    nhb, nhalf = (nxt % 64) // 2, nxt % 2
                    wait("sync", s_pe, half_done)
                    wout_vals.append(run("sync", load_wout_half(nhb, nhalf, buf),
                                         inc=s_wout, amt=16))
            pe_o = s_pe.n
            wait("scalar", s_pe, pe_o)
            if c == 0 and hb == 0 and "psout" in taps and "psout" not in tap_done:
                tap_done.add("psout")
                run("scalar", lambda e, bank=bank: e.activation(
                    out=acc_sb[:], in_=ps[bank][:], func=AF.Copy), inc=s_ac)
                wait("sync", s_ac, s_ac.n)
                run("sync", lambda e: e.dma_start(out=tap_aps["psout"], in_=acc_sb[:]), inc=s_st, amt=16)
                em("scalar", lambda e, s=s_st.h, vv=s_st.n: e.wait_ge(s, vv))
            if hb >= 2:
                wait("scalar", s_shtl, att_store[hb - 2])
            ev = run("scalar", lambda e, hb=hb, bank=bank: e.activation(
                out=attn_st[:, hb % 2, :], in_=ps[bank][:], func=AF.Copy), inc=s_ac)
            bank_guard[bank] = (s_ac, ev)
            wait("sync", s_ac, s_ac.n)
            att_store.append(run("sync", lambda e, c=c, hb=hb: e.dma_start(
                out=attn_in[c].rearrange("(qq hb2 p) t -> p hb2 qq t", p=128, qq=4)[:, hb, :, :],
                in_=attn_st[:, hb % 2, :].rearrange("p (qq t) -> p qq t", qq=4)),
                inc=s_shtl, amt=16))
        wait("gpsimd", s_shtl, s_shtl.n)
        run("gpsimd", lambda e, c=c: e.collective_compute(
            "ReduceScatter", ALU.add, replica_groups=GROUPS,
            ins=[attn_in[c].opt()], outs=[attn_rs[c].opt()]), inc=s_cc)
    cc_all = s_cc.n
    if "woutbf" in taps:
        wait("sync", s_wprep, 32)
        run("sync", lambda e: e.dma_start(out=tap_aps["woutbf"], in_=w_outT_bf[0:2048, 0:128]), inc=s_st, amt=16)
    if "attnp" in taps:
        wait("sync", s_cc, cc_all)
        run("sync", lambda e: e.dma_start(out=tap_aps["attnp"], in_=attn_in[0]), inc=s_st, amt=16)
    if "attnq" in taps:
        wait("sync", s_cc, cc_all)
        run("sync", lambda e: e.dma_start(out=tap_aps["attnq"], in_=attn_rs[0]), inc=s_st, amt=16)
    esA.close()

    with ExitStack() as es2:
        sb2 = lambda name, shape, dt: es2.enter_context(nc.sbuf_tensor(name, list(shape), dt))
        at_sb = sb2("at_sb", [128, 32, 2, 128], F32)
        xm_sb = sb2("xm_sb", [128, 2, H], F32)
        fz_sb = sb2("fz_sb", [128, 2, H], F32)
        st2_sb = sb2("st2_sb", [128, 8], F32)
        ln2_sb = sb2("ln2_sb", [128, 2, H], BF16)
        ln2T_sb = sb2("ln2T_sb", [128, 32, 256], BF16)
        wr_sb = sb2("wr_sb", [128, 32, 64], BF16)
        scf_sb = sb2("scf_sb", [64, 256], F32)
        sctm_sb = sb2("sctm_sb", [128, 2, 64], F32)
        mx_sb = sb2("mx_sb", [128, 2, 8], F32)
        mx2_sb = sb2("mx2_sb", [128, 2, 8], F32)
        scr2_sb = sb2("scr2_sb", [128, 64], F32)
        msk_sb = sb2("msk_sb", [128, 64], F32)
        gt_sb = sb2("gt_sb", [128, 2, 64], BF16)
        gT_sb = sb2("gT_sb", [64, 2, 128], BF16)
        grep_sb = sb2("grep_sb", [128, 8, 256], BF16)
        hg_sb = sb2("hg_sb", [128, 8, 256], BF16)
        asl_sb = sb2("asl_sb", [128, 2, 32, 128], BF16)
        bsl_sb = sb2("bsl_sb", [128, 2, 8, 128], BF16)
        ffn_f = sb2("ffn_f", [128, 4, 256], F32)

        wait("sync", s_cc, cc_all)
        run("sync", lambda e: e.dma_start(
            out=wr_sb[:], in_=ffn_wrT_bf.rearrange("p (hb k) -> p hb k", hb=32)),
            inc=s_dma, amt=16)
        dma_wr = s_dma.n

        n_blocks = (n_chunks + 1) // 2
        asl_vals = []
        bsl_vals = []

        def load_asl(krb, buf):
            def f(e):
                return e.dma_start(
                    out=asl_sb[:, buf, :, :],
                    in_=ffn_aT_bf[128 * krb:128 * (krb + 1), :].rearrange(
                        "p (hb k) -> p hb k", hb=32))
            return f

        def load_bsl(hb, buf):
            def f(e):
                return e.dma_start(
                    out=bsl_sb[:, buf, :, :],
                    in_=ffn_bT_bf[128 * hb:128 * (hb + 1), :].rearrange(
                        "p (krb h) -> p krb h", krb=8))
            return f

        asl_vals.append(run("sync", load_asl(0, 0), inc=s_win, amt=16))
        bsl_vals.append(run("sync", load_bsl(0, 0), inc=s_wout, amt=16))

        for bk in range(n_blocks):
            cids = [2 * bk + i for i in range(2) if 2 * bk + i < n_chunks]
            ncid = len(cids)
            tw = 128 * ncid
            wait("sync", s_cc, cc_all)
            if bk > 0:
                wait("sync", s_dv, v_out_prev)
                wait("sync", s_pe, s_pe.n)
            for ci, c in enumerate(cids):
                run("sync", lambda e, ci=ci, c=c: e.dma_start(
                    out=at_sb[:, :, ci, :],
                    in_=attn_rs[c].rearrange("(hb p) t -> p hb t", p=128)),
                    inc=s_dma, amt=16)
                run("sync", lambda e, ci=ci, c=c: e.dma_start(
                    out=xm_sb[:, ci, :], in_=x_own[128 * c:128 * (c + 1), :]),
                    inc=s_dma, amt=16)
            dma_at = s_dma.n
            wait("tensor", s_dma, dma_at)
            for ci in range(ncid):
                for hg in range(8):
                    bank = hg % 2
                    pe_use_bank(bank)
                    for k in range(4):
                        hb = 4 * hg + k
                        run("tensor", lambda e, ci=ci, hb=hb, k=k, bank=bank: e.transpose(
                            ps[bank][:, 128 * k:128 * (k + 1)],
                            at_sb[:, hb, ci, :], identf[:]),
                            inc=(s_pe if k == 3 else None))
                    pt = s_pe.n
                    wait("scalar", s_pe, pt)
                    ev = run("scalar", lambda e, ci=ci, hg=hg, bank=bank: e.activation(
                        out=fz_sb[:, ci, 512 * hg:512 * (hg + 1)], in_=ps[bank][:],
                        func=AF.Copy), inc=s_ac)
                    bank_guard[bank] = (s_ac, ev)
            ac_att = s_ac.n
            wait("vector", s_ac, ac_att)
            wait("vector", s_dma, dma_at)
            for ci in range(ncid):
                run("vector", lambda e, ci=ci: e.tensor_tensor(
                    out=xm_sb[:, ci, :], in0=xm_sb[:, ci, :], in1=fz_sb[:, ci, :],
                    op=ALU.add), inc=s_dv)
                vx = s_dv.n
                wait("vector", s_dv, vx)
                run("vector", lambda e, ci=ci: e.tensor_reduce(
                    out=st2_sb[:, 0:1], in_=xm_sb[:, ci, :], axis=AX.X, op=ALU.add), inc=s_dv)
                wait("scalar", s_dv, vx)
                run("scalar", lambda e, ci=ci: e.activation(
                    out=fz_sb[:, ci, :], in_=xm_sb[:, ci, :], func=AF.Square), inc=s_ac)
                va = s_ac.n
                wait("vector", s_ac, va)
                run("vector", lambda e, ci=ci: e.tensor_reduce(
                    out=st2_sb[:, 1:2], in_=fz_sb[:, ci, :], axis=AX.X, op=ALU.add), inc=s_dv)
                vr = s_dv.n
                wait("vector", s_dv, vr)
                run("vector", lambda e: e.tensor_scalar(
                    out=st2_sb[:, 2:3], in0=st2_sb[:, 0:1], scalar1=1.0 / H, scalar2=None, op0=ALU.mult), inc=s_dv)
                run("vector", lambda e: e.tensor_scalar(
                    out=st2_sb[:, 4:5], in0=st2_sb[:, 1:2], scalar1=1.0 / H, scalar2=None, op0=ALU.mult), inc=s_dv)
                vr2 = s_dv.n
                wait("vector", s_dv, vr2)
                run("vector", lambda e: e.tensor_tensor(
                    out=st2_sb[:, 3:4], in0=st2_sb[:, 2:3], in1=st2_sb[:, 2:3], op=ALU.mult), inc=s_dv)
                vr3 = s_dv.n
                wait("vector", s_dv, vr3)
                run("vector", lambda e: e.tensor_tensor(
                    out=st2_sb[:, 5:6], in0=st2_sb[:, 4:5], in1=st2_sb[:, 3:4], op=ALU.subtract), inc=s_dv)
                vr4 = s_dv.n
                wait("scalar", s_dv, vr4)
                run("scalar", lambda e: e.activation(
                    out=st2_sb[:, 6:7], in_=st2_sb[:, 5:6], func=AF.Sqrt, bias=cst_sb[:, 0:1]), inc=s_ac)
                va2 = s_ac.n
                wait("vector", s_ac, va2)
                run("vector", lambda e: e.reciprocal(out=st2_sb[:, 7:8], in_=st2_sb[:, 6:7]), inc=s_dv)
                vr5 = s_dv.n
                wait("vector", s_dv, vr5)
                run("vector", lambda e, ci=ci: e.tensor_scalar(
                    out=ln2_sb[:, ci, :], in0=xm_sb[:, ci, :], scalar1=st2_sb[:, 2:3],
                    scalar2=st2_sb[:, 7:8], op0=ALU.subtract, op1=ALU.mult), inc=s_dv)
            v_ln2 = s_dv.n
            wait("tensor", s_dv, v_ln2)
            for j in range(32):
                bank = j % 2
                pe_use_bank(bank)
                for ci in range(ncid):
                    run("tensor", lambda e, ci=ci, j=j, bank=bank: e.transpose(
                        ps[bank][:].bitcast(BF16)[:, 128 * ci:128 * (ci + 1)],
                        ln2_sb[:, ci, 128 * j:128 * (j + 1)], ident[:]),
                        inc=(s_pe if ci == ncid - 1 else None))
                pj = s_pe.n
                wait("scalar", s_pe, pj)
                ev = run("scalar", lambda e, j=j, bank=bank, tw=tw: e.activation(
                    out=ln2T_sb[:, j, 0:tw],
                    in_=ps[bank][:].bitcast(BF16)[:, 0:tw], func=AF.Copy), inc=s_ac)
                bank_guard[bank] = (s_ac, ev)
            ac_lnT = s_ac.n
            pe_use_bank(2)
            wait("tensor", s_ac, ac_lnT)
            wait("tensor", s_dma, dma_wr)
            for hb in range(32):
                run("tensor", lambda e, hb=hb, tw=tw: e.matmul(
                    ps[2][0:64, 0:tw], wr_sb[:, hb, :], ln2T_sb[:, hb, 0:tw],
                    start=(hb == 0), stop=(hb == 31)), inc=(s_pe if hb == 31 else None))
            p_sc = s_pe.n
            wait("scalar", s_pe, p_sc)
            ev = run("scalar", lambda e, tw=tw: e.activation(
                out=scf_sb[:, 0:tw], in_=ps[2][0:64, 0:tw], func=AF.Copy), inc=s_ac)
            bank_guard[2] = (s_ac, ev)
            ac_sc = s_ac.n
            pe_use_bank(3)
            wait("tensor", s_ac, ac_sc)
            for ci in range(ncid):
                run("tensor", lambda e, ci=ci: e.transpose(
                    ps[3][:, 64 * ci:64 * (ci + 1)],
                    scf_sb[0:64, 128 * ci:128 * (ci + 1)], identf[0:64, 0:64]), inc=s_pe)
            p_st = s_pe.n
            wait("scalar", s_pe, p_st)
            ev = run("scalar", lambda e, ncid=ncid: e.activation(
                out=sctm_sb[:].rearrange("p a k -> p (a k)")[:, 0:64 * ncid],
                in_=ps[3][:, 0:64 * ncid], func=AF.Copy), inc=s_ac)
            bank_guard[3] = (s_ac, ev)
            ac_stm = s_ac.n
            for ci in range(ncid):
                wait("vector", s_ac, ac_stm)
                run("vector", lambda e, ci=ci: e.max(
                    out=mx_sb[:, ci, :], in_=sctm_sb[:, ci, :]), inc=s_dv)
                v_m1 = s_dv.n
                wait("vector", s_dv, v_m1)
                run("vector", lambda e, ci=ci: e.match_replace(
                    out=scr2_sb[:], in_to_replace=mx_sb[:, ci, :],
                    in_values=sctm_sb[:, ci, :], imm_value=-1e30), inc=s_dv)
                v_m2 = s_dv.n
                wait("vector", s_dv, v_m2)
                run("vector", lambda e, ci=ci: e.max(
                    out=mx2_sb[:, ci, :], in_=scr2_sb[:]), inc=s_dv)
                v_m3 = s_dv.n
                wait("vector", s_dv, v_m3)
                run("vector", lambda e, ci=ci: e.tensor_single_scalar(
                    out=msk_sb[:], in_=sctm_sb[:, ci, :],
                    scalar=mx2_sb[:, ci, 7:8], op=ALU.is_ge), inc=s_dv)
                run("vector", lambda e, ci=ci: e.tensor_scalar(
                    out=st2_sb[:, 0:1], in0=mx_sb[:, ci, 0:1], scalar1=-1.0,
                    scalar2=None, op0=ALU.mult), inc=s_dv)
                v_m4 = s_dv.n
                wait("scalar", s_dv, v_m4)
                run("scalar", lambda e, ci=ci: e.activation(
                    out=scr2_sb[:], in_=sctm_sb[:, ci, :], func=AF.Exp,
                    bias=st2_sb[:, 0:1]), inc=s_ac)
                va3 = s_ac.n
                wait("vector", s_ac, va3)
                run("vector", lambda e: e.tensor_tensor(
                    out=scr2_sb[:], in0=scr2_sb[:], in1=msk_sb[:], op=ALU.mult), inc=s_dv)
                v_m5 = s_dv.n
                wait("vector", s_dv, v_m5)
                run("vector", lambda e: e.tensor_reduce(
                    out=st2_sb[:, 1:2], in_=scr2_sb[:], axis=AX.X, op=ALU.add), inc=s_dv)
                v_m6 = s_dv.n
                wait("vector", s_dv, v_m6)
                run("vector", lambda e: e.reciprocal(out=st2_sb[:, 2:3], in_=st2_sb[:, 1:2]), inc=s_dv)
                v_m7 = s_dv.n
                wait("vector", s_dv, v_m7)
                run("vector", lambda e: e.tensor_scalar(
                    out=st2_sb[:, 3:4], in0=st2_sb[:, 2:3], scalar1=2.0,
                    scalar2=None, op0=ALU.mult), inc=s_dv)
                v_m8 = s_dv.n
                wait("vector", s_dv, v_m8)
                run("vector", lambda e, ci=ci: e.tensor_scalar(
                    out=gt_sb[:, ci, :], in0=scr2_sb[:], scalar1=st2_sb[:, 3:4],
                    scalar2=None, op0=ALU.mult), inc=s_dv)
            v_gt = s_dv.n
            if bk == 0 and "sct" in taps and "sct" not in tap_done:
                tap_done.add("sct")
                wait("sync", s_dv, v_gt)
                run("sync", lambda e: e.dma_start(out=tap_aps["sct"], in_=sctm_sb[:]), inc=s_st, amt=16)
                run("sync", lambda e: e.dma_start(out=tap_aps["gts"], in_=gt_sb[:]), inc=s_st, amt=16)
                em("vector", lambda e, s=s_st.h, vv=s_st.n: e.wait_ge(s, vv))
            pe_use_bank(3)
            wait("tensor", s_dv, v_gt)
            for ci in range(ncid):
                run("tensor", lambda e, ci=ci: e.transpose(
                    ps[3][:].bitcast(BF16)[0:64, 128 * ci:128 * (ci + 1)],
                    gt_sb[:, ci, :], ident[:]), inc=s_pe)
            p_gt = s_pe.n
            wait("scalar", s_pe, p_gt)
            ev = run("scalar", lambda e, tw=tw: e.activation(
                out=gT_sb[:].rearrange("k a t -> k (a t)")[:, 0:tw],
                in_=ps[3][:].bitcast(BF16)[0:64, 0:tw], func=AF.Copy), inc=s_ac)
            bank_guard[3] = (s_ac, ev)
            ac_gT = s_ac.n
            for kb in range(8):
                bank = kb % 2
                pe_use_bank(bank)
                wait("tensor", s_ac, ac_gT)
                run("tensor", lambda e, kb=kb, tw=tw: e.matmul(
                    ps[kb % 2][:, 0:tw], p8mats[:, kb, :],
                    gT_sb[:].rearrange("k a t -> k (a t)")[:, 0:tw],
                    start=True, stop=True), inc=s_pe)
                pk = s_pe.n
                wait("scalar", s_pe, pk)
                ev = run("scalar", lambda e, kb=kb, bank=bank, tw=tw: e.activation(
                    out=grep_sb[:, kb, 0:tw], in_=ps[bank][:, 0:tw],
                    func=AF.Copy), inc=s_ac)
                bank_guard[bank] = (s_ac, ev)
            ac_grep = s_ac.n
            for krb in range(8):
                bank = 2 + (krb % 2)
                pe_use_bank(bank)
                kk = 8 * bk + krb
                buf = kk % 2
                wait("tensor", s_win, asl_vals[kk])
                for hb in range(32):
                    run("tensor", lambda e, hb=hb, buf=buf, bank=bank, tw=tw: e.matmul(
                        ps[bank][:, 0:tw], asl_sb[:, buf, hb, :],
                        ln2T_sb[:, hb, 0:tw], start=(hb == 0), stop=(hb == 31)),
                        inc=(s_pe if hb == 31 else None))
                ph = s_pe.n
                nxtk = kk + 1
                if nxtk < 8 * n_blocks:
                    wait("sync", s_pe, ph)
                    asl_vals.append(run("sync", load_asl(nxtk % 8, nxtk % 2),
                                        inc=s_win, amt=16))
                wait("vector", s_pe, ph)
                wait("vector", s_ac, ac_grep)
                ev = run("vector", lambda e, krb=krb, bank=bank, tw=tw: e.tensor_tensor(
                    out=hg_sb[:, krb, 0:tw], in0=ps[bank][:, 0:tw],
                    in1=grep_sb[:, krb, 0:tw], op=ALU.mult), inc=s_dv)
                bank_guard[bank] = (s_dv, ev)
            v_hg = s_dv.n
            if bk == 0 and "hg" in taps and "hg" not in tap_done:
                tap_done.add("hg")
                wait("sync", s_dv, v_hg)
                run("sync", lambda e: e.dma_start(out=tap_aps["hg"], in_=hg_sb[:]), inc=s_st, amt=16)
                run("sync", lambda e: e.dma_start(out=tap_aps["grep"], in_=grep_sb[:]), inc=s_st, amt=16)
            for hb in range(32):
                bank = 4 + (hb % 2)
                pe_use_bank(bank)
                kk = 32 * bk + hb
                buf = kk % 2
                wait("tensor", s_wout, bsl_vals[kk])
                wait("tensor", s_dv, v_hg)
                for krb in range(8):
                    run("tensor", lambda e, krb=krb, buf=buf, bank=bank, tw=tw: e.matmul(
                        ps[bank][:, 0:tw], bsl_sb[:, buf, krb, :],
                        hg_sb[:, krb, 0:tw], start=(krb == 0), stop=(krb == 7)),
                        inc=(s_pe if krb == 7 else None))
                pf = s_pe.n
                nxtk = kk + 1
                if nxtk < 32 * n_blocks:
                    wait("sync", s_pe, pf)
                    bsl_vals.append(run("sync", load_bsl(nxtk % 32, nxtk % 2),
                                        inc=s_wout, amt=16))
                wait("scalar", s_pe, pf)
                ev = run("scalar", lambda e, hb=hb, bank=bank, tw=tw: e.activation(
                    out=ffn_f[:, hb % 4, 0:tw], in_=ps[bank][:, 0:tw],
                    func=AF.Copy), inc=s_ac)
                bank_guard[bank] = (s_ac, ev)
                if hb % 4 == 3:
                    hb0 = hb - 3
                    ac_f = s_ac.n
                    for ci in range(ncid):
                        bank2 = 6 + (ci % 2)
                        pe_use_bank(bank2)
                        wait("tensor", s_ac, ac_f)
                        for k in range(4):
                            run("tensor", lambda e, k=k, ci=ci, bank2=bank2: e.transpose(
                                ps[bank2][:, 128 * k:128 * (k + 1)],
                                ffn_f[:, k, 128 * ci:128 * (ci + 1)], identf[:]),
                                inc=(s_pe if k == 3 else None))
                        pt2 = s_pe.n
                        wait("scalar", s_pe, pt2)
                        ev = run("scalar", lambda e, ci=ci, hb0=hb0, bank2=bank2: e.activation(
                            out=fz_sb[:, ci, 128 * hb0:128 * (hb0 + 4)],
                            in_=ps[bank2][:], func=AF.Copy), inc=s_ac)
                        bank_guard[bank2] = (s_ac, ev)
            ac_ffn = s_ac.n
            wait("vector", s_ac, ac_ffn)
            for ci in range(ncid):
                run("vector", lambda e, ci=ci: e.tensor_tensor(
                    out=xm_sb[:, ci, :], in0=xm_sb[:, ci, :], in1=fz_sb[:, ci, :],
                    op=ALU.add), inc=s_dv)
            v_out = s_dv.n
            v_out_prev = v_out
            wait("sync", s_dv, v_out)
            for ci, c in enumerate(cids):
                run("sync", lambda e, ci=ci, c=c: e.dma_start(
                    out=y_out[128 * c:128 * (c + 1), :], in_=xm_sb[:, ci, :]),
                    inc=s_st, amt=16)

    wait("sync", s_st, s_st.n)

    with nc.Block() as block:
        @block.sync
        def _(eng):
            for f in plan["sync"]:
                f(eng)

        @block.vector
        def _(eng):
            for f in plan["vector"]:
                f(eng)

        @block.scalar
        def _(eng):
            for f in plan["scalar"]:
                f(eng)

        @block.tensor
        def _(eng):
            for f in plan["tensor"]:
                f(eng)

        @block.gpsimd
        def _(eng):
            for f in plan["gpsimd"]:
                f(eng)

    es.close()
    return nc


def make_in_maps(inputs, n_chunks=L // TC):
    x = np.asarray(inputs["L0_attn_hidden_states"])
    W_in = np.asarray(inputs["L0_attn_W_in"])
    conv_w_np = np.asarray(inputs["L0_attn_conv_w"])
    conv_b_np = np.asarray(inputs["L0_attn_conv_b"])
    Wx = np.asarray(inputs["L0_attn_Wx"])
    dt_ln_w_np = np.asarray(inputs["L0_attn_dt_ln_w"])
    b_ln_w_np = np.asarray(inputs["L0_attn_b_ln_w"])
    c_ln_w_np = np.asarray(inputs["L0_attn_c_ln_w"])
    Wdt = np.asarray(inputs["L0_attn_Wdt"])
    dt_bias_np = np.asarray(inputs["L0_attn_dt_bias"])
    A = np.asarray(inputs["L0_attn_A"])
    D_param = np.asarray(inputs["L0_attn_D_param"])
    W_out = np.asarray(inputs["L0_attn_W_out"])
    fA = np.asarray(inputs["L0_ffn_A"])
    fB = np.asarray(inputs["L0_ffn_B"])
    fWr = np.asarray(inputs["L0_ffn_Wr"])

    C = np.ascontiguousarray
    in_maps = []
    for core in range(NCORES):
        b, r = core // 4, core % 4
        dsl = slice(DC * r, DC * (r + 1))
        rows = []
        for c in range(n_chunks):
            t0 = TC * c + 128 * r
            rows.append(x[b, t0:t0 + 128, :])
        x_own_np = C(np.concatenate(rows, 0).astype(np.float32))
        w_in_sl = np.concatenate([W_in[dsl], W_in[D + DC * r: D + DC * (r + 1)]], 0)
        def pmaj(v, nb):
            return C(np.ascontiguousarray(np.swapaxes(v.reshape(nb, 128, *v.shape[1:]), 0, 1)).reshape(v.shape))
        m = {
            "x_own": x_own_np,
            "w_inT": C(w_in_sl.T.reshape(2, 16, 128, 32, 128)
                       .transpose(3, 0, 2, 1, 4).reshape(8192, 2048)),
            "w_outT": C(W_out[:, dsl].T.reshape(2, 8, 128, 32, 128)
                        .transpose(3, 0, 2, 1, 4).reshape(8192, 1024)),
            "wxT": pmaj(C(Wx[:, dsl].T), NDB),
            "wdtT": pmaj(C(Wdt[dsl].T), 2),
            "conv_w": pmaj(C(conv_w_np[dsl]), NDB),
            "conv_b": pmaj(C(conv_b_np[dsl]), NDB),
            "a_par": pmaj(C(A[dsl]), NDB),
            "d_par": pmaj(C(D_param[dsl]), NDB),
            "dt_bias": pmaj(C(dt_bias_np[dsl]), NDB),
            "dt_ln_w": pmaj(C(dt_ln_w_np), 2),
            "b_ln_w": C(b_ln_w_np),
            "c_ln_w": C(c_ln_w_np),
            "ffn_aT": C(fA.reshape(1024, H).T.reshape(32, 128, 8, 128)
                        .transpose(2, 1, 0, 3).reshape(1024, 4096)),
            "ffn_bT": C(fB.transpose(0, 2, 1).reshape(1024, H)
                        .reshape(8, 128, 32, 128)
                        .transpose(2, 1, 0, 3).reshape(4096, 1024)),
            "ffn_wrT": C(fWr.T.reshape(32, 128, 64)
                         .transpose(1, 0, 2).reshape(128, 2048)),
        }
        in_maps.append(m)
    return in_maps


def assemble_out(results, n_chunks=L // TC):
    out = np.zeros((BATCH, L, H), np.float32)
    for core in range(NCORES):
        b, r = core // 4, core % 4
        yo = results[core]["y_out"]
        for c in range(n_chunks):
            t0 = TC * c + 128 * r
            out[b, t0:t0 + 128, :] = yo[128 * c:128 * (c + 1), :]
    return out


def kernel(**inputs):
    n_chunks = L // TC
    nc = build_program(n_chunks=n_chunks)
    in_maps = make_in_maps(inputs, n_chunks)
    res = run_bass_kernel_spmd(nc, in_maps, core_ids=list(range(NCORES)))
    return assemble_out(res.results, n_chunks)
